# version 06
# baseline (speedup 1.0000x reference)
import numpy as np
import ml_dtypes
from contextlib import ExitStack

import concourse.bass as bass
import concourse.mybir as mybir
from concourse.bass_utils import run_bass_kernel_spmd

F32 = mybir.dt.float32
BF16 = mybir.dt.bfloat16
AF = mybir.ActivationFunctionType
ALU = mybir.AluOpType
AX = mybir.AxisListType

P = 128
D = 2048
NCH = 16
NT = 32
NTA = 64
NST = 4
NTH = 4
TH = 256
NTS = 8
NE = 32
GCH = 6
EPS = 1e-5
RD = 1.0 / D


class Sem:
    def __init__(self, ctx, nc, name):
        self.h = ctx.enter_context(nc.semaphore(name))
        self.v = 0


def build_nc(stage=3):
    nc = bass.Bass("TRN2", target_bir_lowering=False, debug=False)
    dt_in = {
        "x_own": ([4096, D], F32),
        "wqk": ([D, D], F32), "wv": ([D, D], F32),
        "wu_t": ([NCH, P, NCH, P], F32), "wo_t": ([NCH, P, NCH, P], F32),
        "gup_t": ([NE, 12, P, NCH, P], F32), "wd_t": ([NE, P, 4, GCH, 512], F32),
        "gate_t": ([P, NCH, 32], F32),
        "bqk_r": ([1, D], F32), "bv_r": ([1, D], F32), "bo_c": ([P, NCH], F32),
        "bu_c": ([P, NCH], F32), "lng_c": ([P, NCH], F32), "lnb_c": ([P, NCH], F32),
        "ones_r16": ([1, P], BF16), "ones_c16": ([P, 1], BF16), "ones_r32": ([1, P], F32),
        "id16": ([P, P], BF16), "id32": ([P, P], F32),
    }
    din = {k: nc.dram_tensor(k, s, d, kind="ExternalInput").ap() for k, (s, d) in dt_in.items()}
    o = nc.dram_tensor("o", [4096, D], F32, kind="ExternalOutput").ap()
    hf_s = nc.dram_tensor("hf_s", [NT, P, D], BF16, kind="Internal").ap()
    kv_cc_in = nc.dram_tensor("kv_cc_in", [P, D], F32, kind="Internal").ap()
    kv_cc_out = nc.dram_tensor("kv_cc_out", [P, D], F32, kind="Internal").ap()
    qf_s = nc.dram_tensor("qf_s", [NT, P, D], BF16, kind="Internal").ap()

    plan = []

    def op(eng, fn, waits=(), incs=()):
        seen = {}
        for s, v in waits:
            if v > 0 and (s not in seen or seen[s] < v):
                seen[s] = v
        for s, k in incs:
            s.v += k
        plan.append((eng, fn, list(seen.items()), list(incs)))

    guards = {}

    def set_guard(key, *sems):
        guards[key] = [(s, s.v) for s in sems]

    def gw(*keys):
        out = []
        for k in keys:
            out.extend(guards.get(k, ()))
        return out

    ctx = ExitStack()
    with ctx:
        sy = Sem(ctx, nc, "sy")
        sst = Sem(ctx, nc, "sst")
        sout = Sem(ctx, nc, "sout")
        gp = Sem(ctx, nc, "gp")
        pe = Sem(ctx, nc, "pe")
        po = Sem(ctx, nc, "po")
        dv = Sem(ctx, nc, "dv")
        ac = Sem(ctx, nc, "ac")
        sx = [Sem(ctx, nc, f"sx{i}") for i in range(2)]
        shfq = Sem(ctx, nc, "shfq")
        sxb = [Sem(ctx, nc, f"sxb{i}") for i in range(2)]
        swu = [Sem(ctx, nc, f"swu{i}") for i in range(2)]
        swg = [Sem(ctx, nc, f"swg{i}") for i in range(2)]
        swd = [Sem(ctx, nc, f"swd{i}") for i in range(2)]

        kv16 = ctx.enter_context(nc.sbuf_tensor("kv16", [P, D], BF16))
        bqk16 = ctx.enter_context(nc.sbuf_tensor("bqk16", [1, D], BF16))
        bv16 = ctx.enter_context(nc.sbuf_tensor("bv16", [1, D], BF16))
        bu_sb = ctx.enter_context(nc.sbuf_tensor("bu_sb", [P, NCH], F32))
        bo_sb = ctx.enter_context(nc.sbuf_tensor("bo_sb", [P, NCH], F32))
        lng_sb = ctx.enter_context(nc.sbuf_tensor("lng_sb", [P, NCH], F32))
        lnb_sb = ctx.enter_context(nc.sbuf_tensor("lnb_sb", [P, NCH], F32))
        ones_r = ctx.enter_context(nc.sbuf_tensor("ones_r", [1, P], BF16))
        ones_c = ctx.enter_context(nc.sbuf_tensor("ones_c", [P, 1], BF16))
        ones32 = ctx.enter_context(nc.sbuf_tensor("ones32", [1, P], F32))
        id16_sb = ctx.enter_context(nc.sbuf_tensor("id16_sb", [P, P], BF16))
        id32_sb = ctx.enter_context(nc.sbuf_tensor("id32_sb", [P, P], F32))
        gate16 = ctx.enter_context(nc.sbuf_tensor("gate16", [P, NCH * 32], BF16))

        for dst, key in [(bu_sb, "bu_c"), (bo_sb, "bo_c"), (lng_sb, "lng_c"),
                         (lnb_sb, "lnb_c"), (ones_r, "ones_r16"), (ones_c, "ones_c16"),
                         (ones32, "ones_r32"), (id16_sb, "id16"), (id32_sb, "id32")]:
            op("sync", lambda e, d=dst, k=key: e.dma_start(out=d[:], in_=din[k]),
               incs=[(sy, 16)])
        for dst, key in [(bqk16, "bqk_r"), (bv16, "bv_r"), (gate16, "gate_t")]:
            op("gpsimd", lambda e, d=dst, k=key: e.dma_start(out=d[:], in_=din[k]),
               incs=[(gp, 16)])
        misc_sy = sy.v

        with ExitStack() as actx:
            warm16 = actx.enter_context(nc.sbuf_tensor("warm16", [P, 512], BF16))
            wqk_sb = actx.enter_context(nc.sbuf_tensor("wqk_sb", [P, NCH * D], BF16))
            wv_sb = actx.enter_context(nc.sbuf_tensor("wv_sb", [P, NCH * D], BF16))
            x_t = [actx.enter_context(nc.sbuf_tensor(f"x_t{i}", [P, D], F32)) for i in range(2)]
            stat = [actx.enter_context(nc.sbuf_tensor(f"stat{i}", [P, 8], F32)) for i in range(2)]
            h16 = [actx.enter_context(nc.sbuf_tensor("h16_0", [P, D], BF16))] * 2
            hf_sb = [actx.enter_context(nc.sbuf_tensor(f"hf_sb{i}", [P, D], BF16)) for i in range(2)]
            q16 = [actx.enter_context(nc.sbuf_tensor(f"q16_{i}", [P, D], BF16)) for i in range(2)]
            v16 = [actx.enter_context(nc.sbuf_tensor("v16_0", [P, D], BF16))]
            kv_acc = actx.enter_context(nc.sbuf_tensor("kv_acc", [P, D], F32))
            qf_sb = [actx.enter_context(nc.sbuf_tensor("qf_sb0", [P, D], BF16))] * 2
            r_sb = [actx.enter_context(nc.sbuf_tensor("r_sb0", [P, 512], F32))]
            m_sb = [actx.enter_context(nc.sbuf_tensor("m_sb0", [P, 512], F32))]
            e_sb = [actx.enter_context(nc.sbuf_tensor("e_sb0", [P, 512], F32))]
            kv_ps = actx.enter_context(nc.psum_tensor("kv_ps", [P, D], F32))
            htr_ps = actx.enter_context(nc.psum_tensor("htr_ps", [P, D], BF16))
            qv_ps = [actx.enter_context(nc.psum_tensor(f"qv_ps{i}", [P, 512], F32)) for i in range(2)]

            for kc in range(NCH):
                op("gpsimd", lambda e, kc=kc: e.dma_start(
                    out=wqk_sb[:, kc * D:(kc + 1) * D], in_=din["wqk"][kc * P:(kc + 1) * P, :]),
                    incs=[(gp, 16)])
            for kc in range(NCH):
                op("gpsimd", lambda e, kc=kc: e.dma_start(
                    out=wv_sb[:, kc * D:(kc + 1) * D], in_=din["wv"][kc * P:(kc + 1) * P, :]),
                    incs=[(gp, 16)])
            wA_gp = gp.v

            for wi in range(40):
                op("tensor", lambda e: e.matmul(
                    qv_ps[0][:], warm16[:, 0:P], warm16[:], start=True, stop=True))

            nqv = 0
            for t in range(NT):
                b = t % 2
                own = True
                src = din["x_own"]
                row0 = t * P

                op("sync", lambda e, s=src, r=row0, b=b: e.dma_start(
                    out=x_t[b][:], in_=s[r:r + P, :]),
                    waits=gw(("x", b)), incs=[(sx[b], 16)])
                xw = [(sx[b], sx[b].v)]

                pro = [(sy, misc_sy)] if t == 0 else []
                op("scalar", lambda e, b=b: e.activation(
                    h16[b][:], x_t[b][:], AF.Copy, bias=0.0, scale=1.0,
                    accum_out=stat[b][:, 0:1]),
                    waits=xw + pro + gw(("stat", b)) + gw(("h16", 0)), incs=[(ac, 1)])
                op("scalar", lambda e, b=b: e.activation(
                    h16[b][:], x_t[b][:], AF.Square, bias=0.0, scale=1.0,
                    accum_out=stat[b][:, 1:2]), incs=[(ac, 1)])
                sq_done = ac.v
                op("vector", lambda e, b=b: e.tensor_scalar(
                    stat[b][:, 2:3], stat[b][:, 0:1], RD, None, ALU.mult),
                    waits=[(ac, sq_done)])
                op("vector", lambda e, b=b: e.tensor_scalar(
                    stat[b][:, 3:4], stat[b][:, 1:2], RD, None, ALU.mult),
                    waits=[(ac, sq_done)])
                op("vector", lambda e, b=b: e.tensor_tensor(
                    stat[b][:, 4:5], stat[b][:, 2:3], stat[b][:, 2:3], ALU.mult))
                op("vector", lambda e, b=b: e.scalar_tensor_tensor(
                    stat[b][:, 5:6], stat[b][:, 3:4], EPS, stat[b][:, 4:5],
                    ALU.add, ALU.subtract), incs=[(dv, 1)])
                var_done = dv.v
                op("scalar", lambda e, b=b: e.activation(
                    stat[b][:, 6:7], stat[b][:, 5:6], AF.Ln, bias=0.0, scale=1.0),
                    waits=[(dv, var_done)])
                op("scalar", lambda e, b=b: e.activation(
                    stat[b][:, 6:7], stat[b][:, 6:7], AF.Exp, bias=0.0, scale=-0.5),
                    incs=[(ac, 1)])
                rstd_done = ac.v
                op("vector", lambda e, b=b: e.scalar_tensor_tensor(
                    stat[b][:, 7:8], stat[b][:, 2:3], -1.0, stat[b][:, 6:7], ALU.mult, ALU.mult),
                    waits=[(ac, rstd_done)])
                op("vector", lambda e, b=b: e.tensor_scalar(
                    h16[b][:], x_t[b][:], stat[b][:, 6:7], stat[b][:, 7:8], ALU.mult, ALU.add),
                    incs=[(dv, 1)])
                apply_done = dv.v
                set_guard(("x", b), dv, ac)
                set_guard(("stat", b), dv)

                for ch in range(NCH):
                    op("tensor", lambda e, b=b, ch=ch: e.transpose(
                        htr_ps[:, ch * P:(ch + 1) * P], h16[b][:, ch * P:(ch + 1) * P], id16_sb[:]),
                        waits=([(dv, apply_done)] + pro + gw("htr")) if ch == 0 else (),
                        incs=([(pe, 1)] if ch == NCH - 1 else ()))
                htr_done = pe.v
                set_guard(("h16", 0), pe)
                op("vector", lambda e, b=b: e.tensor_copy(hf_sb[b][:], htr_ps[:]),
                   waits=[(pe, htr_done)] + gw(("hf", b)), incs=[(dv, 1)])
                set_guard("htr", dv)
                hf_done = dv.v
                if own:
                    op("sync", lambda e, t=t, b=b: e.dma_start(out=hf_s[t], in_=hf_sb[b][:]),
                       waits=[(dv, hf_done)], incs=[(sst, 16)])
                hf_store_v = sst.v

                bv_ = 0
                for which, wsb, brow in (("q", wqk_sb, bqk16), ("v", wv_sb, bv16)):
                    for no in range(4):
                        slot = nqv % 2
                        nqv += 1
                        ws = [(dv, hf_done)] + gw(("qv", slot))
                        if t == 0:
                            ws.append((gp, wA_gp))
                        op("tensor", lambda e, slot=slot, brow=brow, no=no: e.matmul(
                            qv_ps[slot][:], ones_r[0:1, :], brow[0:1, no * 512:(no + 1) * 512],
                            start=True, stop=False), waits=ws)
                        for kc in range(NCH):
                            op("tensor", lambda e, slot=slot, b=b, kc=kc, wsb=wsb, no=no: e.matmul(
                                qv_ps[slot][:], hf_sb[b][:, kc * P:(kc + 1) * P],
                                wsb[:, kc * D + no * 512: kc * D + (no + 1) * 512],
                                start=False, stop=(kc == NCH - 1)),
                                incs=([(pe, 1)] if kc == NCH - 1 else ()))
                        mm_done = pe.v
                        if which == "q":
                            e2 = 0
                            op("scalar", lambda e, slot=slot, e2=e2: e.activation(
                                r_sb[e2][:], qv_ps[slot][:], AF.Relu, bias=0.0, scale=1.0),
                                waits=[(pe, mm_done)] + gw(("rm", e2)))
                            op("scalar", lambda e, slot=slot, e2=e2: e.activation(
                                m_sb[e2][:], qv_ps[slot][:], AF.Relu, bias=0.0, scale=-1.0),
                                incs=[(ac, 1)])
                            set_guard(("qv", slot), ac)
                            op("scalar", lambda e, e2=e2: e.activation(
                                e_sb[e2][:], m_sb[e2][:], AF.Exp, bias=0.0, scale=-1.0),
                                waits=gw(("e", e2)), incs=[(ac, 1)])
                            e_done = ac.v
                            wq = [(ac, e_done)] + (gw(("q16", b)) if no == 0 else [])
                            op("vector", lambda e, b=b, e2=e2, no=no: e.scalar_tensor_tensor(
                                q16[b][:, no * 512:(no + 1) * 512], e_sb[e2][:], -1.0, r_sb[e2][:],
                                ALU.add, ALU.add), waits=wq, incs=[(po, 1)])
                            set_guard(("rm", e2), po)
                            set_guard(("e", e2), po)
                        else:
                            wv_ = [(pe, mm_done)] + (gw(("v16", 0)) if no == 0 else [])
                            op("scalar", lambda e, slot=slot, no=no: e.activation(
                                v16[0][:, no * 512:(no + 1) * 512], qv_ps[slot][:], AF.Silu,
                                bias=0.0, scale=1.0), waits=wv_, incs=[(ac, 1)])
                            set_guard(("qv", slot), ac)
                q_done_po = po.v
                v_done_ac = ac.v

                for h in range(NCH):
                    ws = ([(po, q_done_po), (ac, v_done_ac)] + gw("kvp")) if h == 0 else ()
                    op("tensor", lambda e, b=b, h=h: e.matmul(
                        kv_ps[:, h * P:(h + 1) * P], q16[b][:, h * P:(h + 1) * P],
                        v16[0][:, h * P:(h + 1) * P],
                        start=True, stop=True),
                        waits=ws, incs=([(pe, 1)] if h == NCH - 1 else ()))
                set_guard(("v16", 0), pe)
                kvmm = pe.v
                if t == 0:
                    op("vector", lambda e: e.tensor_copy(kv_acc[:], kv_ps[:]),
                       waits=[(pe, kvmm)], incs=[(dv, 1)])
                else:
                    op("vector", lambda e: e.tensor_tensor(
                        kv_acc[:], kv_ps[:], kv_acc[:], ALU.add),
                        waits=[(pe, kvmm)], incs=[(dv, 1)])
                set_guard("kvp", dv)

                if own:
                    for ch in range(NCH):
                        op("tensor", lambda e, b=b, ch=ch: e.transpose(
                            htr_ps[:, ch * P:(ch + 1) * P], q16[b][:, ch * P:(ch + 1) * P],
                            id16_sb[:]),
                            waits=(([(po, q_done_po)] + gw("htr")) if ch == 0 else ()),
                            incs=([(pe, 1)] if ch == NCH - 1 else ()))
                    qtr_done = pe.v
                    op("vector", lambda e, b=b: e.tensor_copy(qf_sb[b][:], htr_ps[:]),
                       waits=[(pe, qtr_done)] + gw(("qf", 0)), incs=[(dv, 1)])
                    set_guard("htr", dv)
                    qf_done = dv.v
                    op("sync", lambda e, t=t, b=b: e.dma_start(out=qf_s[t], in_=qf_sb[b][:]),
                       waits=[(dv, qf_done)], incs=[(sst, 16)])
                    set_guard(("qf", 0), sst)
                set_guard(("q16", b), pe)
                guards[("hf", b)] = [(pe, pe.v), (sst, hf_store_v)]

            scc = Sem(ctx, nc, "scc")
            scc2 = Sem(ctx, nc, "scc2")
            op("gpsimd", lambda e: e.dma_start(out=kv_cc_in, in_=kv_acc[:]),
               waits=[(dv, dv.v)], incs=[(scc, 16)])
            op("gpsimd", lambda e: e.collective_compute(
                "AllReduce", mybir.AluOpType.add,
                replica_groups=[[0, 1], [2, 3], [4, 5], [6, 7]],
                ins=[kv_cc_in.opt()], outs=[kv_cc_out.opt()]),
                waits=[(scc, 16)], incs=[(scc2, 1)])
            op("gpsimd", lambda e: e.dma_start(out=kv_acc[:], in_=kv_cc_out),
               waits=[(scc2, 1)], incs=[(scc, 16)])
            op("vector", lambda e: e.tensor_copy(kv16[:], kv_acc[:]),
               waits=[(scc, 32)], incs=[(dv, 1)])
            kv_done = dv.v
        barrier_A = [(dv, kv_done), (sst, sst.v)]

        if stage == 1:
            op("sync", lambda e: e.dma_start(out=o[0:P, 0:D // 2], in_=kv16.bitcast(F32)[:, 0:D // 2]),
               waits=list(barrier_A) + [(dv, kv_done)], incs=[(sout, 16)])
            op("sync", lambda e: e.nop(), waits=[(sout, sout.v)])
            with nc.Block() as block:
                def runner(eng_name):
                    def f(eng):
                        for e_, fn, ws, incs in plan:
                            if e_ != eng_name:
                                continue
                            for s_, v in ws:
                                eng.wait_ge(s_.h, v)
                            inst = fn(eng)
                            for s_, k in incs:
                                inst.then_inc(s_.h, k)
                    return f
                block.sync(runner("sync"))
                block.gpsimd(runner("gpsimd"))
                block.tensor(runner("tensor"))
                block.vector(runner("vector"))
                block.scalar(runner("scalar"))
            return nc

        with ExitStack() as bctx:
            ACC = bctx.enter_context(nc.sbuf_tensor("ACC", [P, NTS * D], F32))
            h2f = bctx.enter_context(nc.sbuf_tensor("h2f", [P, NTS * D], BF16))
            xB = [bctx.enter_context(nc.sbuf_tensor(f"xB{i}", [P, D], F32)) for i in range(2)]
            stat2 = bctx.enter_context(nc.sbuf_tensor("stat2", [P, 8], F32))
            h2t16 = bctx.enter_context(nc.sbuf_tensor("h2t16", [P, D], BF16))
            astat = bctx.enter_context(nc.sbuf_tensor("astat", [1, TH * 6], F32))
            w_sb = bctx.enter_context(nc.sbuf_tensor("w_sb", [P, NTS * 32], F32))
            rw_sb = bctx.enter_context(nc.sbuf_tensor("rw_sb", [P, 32], F32))
            rwm_sb = bctx.enter_context(nc.sbuf_tensor("rwm_sb", [P, 32], F32))
            mk_sb = bctx.enter_context(nc.sbuf_tensor("mk_sb", [P, 32], F32))
            msc = bctx.enter_context(nc.sbuf_tensor("msc", [P, 8], F32))
            ztmp = bctx.enter_context(nc.sbuf_tensor("ztmp", [P, TH], BF16))

            first_sync = [True]
            first_gp = [True]
            first_pe = [True]

            nslice = 0
            ngup = 0
            for st in range(NST):
                sctx = ExitStack()
                hfB = sctx.enter_context(nc.sbuf_tensor(f"hfB{st}", [P, 2 * D], BF16))
                qfB = sctx.enter_context(nc.sbuf_tensor(f"qfB{st}", [P, 2 * D], BF16))
                u16 = sctx.enter_context(nc.sbuf_tensor(f"u16_{st}", [P, NCH * TH], BF16))
                a16 = sctx.enter_context(nc.sbuf_tensor(f"a16_{st}", [P, NCH * TH], BF16))
                sq16 = sctx.enter_context(nc.sbuf_tensor(f"sq16_{st}", [P, NCH * TH], BF16))
                outf = sctx.enter_context(nc.sbuf_tensor(f"outf{st}", [P, NCH * TH], F32))
                wu_sl = [sctx.enter_context(nc.sbuf_tensor(f"wu_sl{st}_{i}", [P, NCH * P], BF16))
                         for i in range(2)]
                attn_entry = ([(pe, pe.v), (dv, dv.v), (ac, ac.v)]
                              if st > 0 else list(barrier_A))
                first_sync_st = [True]
                first_gp_st = [True]

                def asy():
                    if first_sync_st[0]:
                        first_sync_st[0] = False
                        return list(attn_entry)
                    return []

                def agp():
                    if first_gp_st[0]:
                        first_gp_st[0] = False
                        return list(attn_entry)
                    return []

                for th in range(NTH):
                    tt0 = st * NTS + th * 2
                    with ExitStack() as pctx:
                        pa_f = [pctx.enter_context(nc.psum_tensor(f"pa_{st}_{th}_{i}", [P, 512], F32))
                                for i in range(2)]
                        pu_f = [pctx.enter_context(nc.psum_tensor(f"pu_{st}_{th}_{i}", [P, 512], F32))
                                for i in range(2)]
                        pa = [t[:, 0:TH] for t in pa_f]
                        pu = [t[:, 0:TH] for t in pu_f]
                        pst_a = pctx.enter_context(nc.psum_tensor(f"psta_{st}_{th}", [1, 512], F32))
                        pst_b = pctx.enter_context(nc.psum_tensor(f"pstb_{st}_{th}", [1, 512], F32))
                        pbc = pctx.enter_context(nc.psum_tensor(f"pbc_{st}_{th}", [P, 2 * TH], F32))

                        ldw = asy() + gw("hfB", "qfB")
                        for i, tt in enumerate((tt0, tt0 + 1)):
                            op("sync", lambda e, i=i, tt=tt, hfB=hfB: e.dma_start(
                                out=hfB[:, i * D:(i + 1) * D], in_=hf_s[tt]),
                                waits=(ldw if i == 0 else ()), incs=[(shfq, 16)])
                            op("sync", lambda e, i=i, tt=tt, qfB=qfB: e.dma_start(
                                out=qfB[:, i * D:(i + 1) * D], in_=qf_s[tt]),
                                incs=[(shfq, 16)])
                        hfld = shfq.v

                        for mc in range(NCH):
                            slot = nslice % 2
                            nslice += 1
                            op("gpsimd", lambda e, slot=slot, mc=mc, wu_sl=wu_sl: e.dma_start(
                                out=wu_sl[slot][:], in_=din["wu_t"][mc]),
                                waits=agp() + gw(("wusl", slot)), incs=[(swu[slot], 16)])
                            wld = swu[slot].v
                            for kc in range(NCH):
                                ws = []
                                if kc == 0:
                                    ws = ([(swu[slot], wld), (shfq, hfld)] + gw(("pu", mc % 2))
                                          + (gw("u16") if mc == 0 else []))
                                    if first_pe[0]:
                                        first_pe[0] = False
                                        ws.append((dv, kv_done))
                                op("tensor", lambda e, slot=slot, mc=mc, kc=kc, wu_sl=wu_sl,
                                   pu=pu, hfB=hfB: e.matmul(
                                    pu[mc % 2][:], wu_sl[slot][:, kc * P:(kc + 1) * P],
                                    hfB.rearrange("p (t c j) -> p t c j", t=2, c=NCH)[:, :, kc, :],
                                    start=(kc == 0), stop=(kc == NCH - 1)),
                                    waits=ws, incs=([(pe, 1)] if kc == NCH - 1 else ()))
                            set_guard(("wusl", slot), pe)
                            umm = pe.v
                            wsilu = [(pe, umm)] + (gw("u16") if mc == 0 else [])
                            op("scalar", lambda e, mc=mc, pu=pu, u16=u16: e.activation(
                                u16[:, mc * TH:(mc + 1) * TH], pu[mc % 2][:], AF.Silu,
                                bias=bu_sb[:, mc:mc + 1], scale=1.0),
                                waits=wsilu, incs=[(ac, 1)])
                            set_guard(("pu", mc % 2), ac)
                        set_guard("hfB", pe)

                        for h in range(NCH):
                            ws = [(shfq, hfld)] + gw(("pa", h % 2))
                            op("tensor", lambda e, h=h, pa=pa, qfB=qfB: e.matmul(
                                pa[h % 2][:], kv16[:, h * P:(h + 1) * P],
                                qfB.rearrange("p (t c j) -> p t c j", t=2, c=NCH)[:, :, h, :],
                                start=True, stop=True), waits=ws, incs=[(pe, 1)])
                            amm = pe.v
                            op("vector", lambda e, h=h, pa=pa, a16=a16: e.tensor_copy(
                                a16[:, h * TH:(h + 1) * TH], pa[h % 2][:]),
                                waits=[(pe, amm)], incs=[(dv, 1)])
                            cpv = dv.v
                            if h % 4 == 3:
                                op("scalar", lambda e, h=h, a16=a16, sq16=sq16: e.activation(
                                    sq16[:, (h - 3) * TH:(h + 1) * TH],
                                    a16[:, (h - 3) * TH:(h + 1) * TH],
                                    AF.Square, bias=0.0, scale=1.0),
                                    waits=[(dv, cpv)], incs=[(ac, 1)])
                            set_guard(("pa", h % 2), dv)
                        evac_dv = dv.v
                        evac_ac = ac.v
                        set_guard("qfB", pe)

                        for h in range(NCH):
                            ws = ([(dv, evac_dv), (ac, evac_ac)] + gw("pst")) if h == 0 else ()
                            op("tensor", lambda e, h=h, pst_a=pst_a, a16=a16: e.matmul(
                                pst_a[0:1, 0:TH], ones_c[:, 0:1], a16[:, h * TH:(h + 1) * TH],
                                start=(h == 0), stop=(h == NCH - 1)), waits=ws)
                            op("tensor", lambda e, h=h, pst_b=pst_b, sq16=sq16: e.matmul(
                                pst_b[0:1, 0:TH], ones_c[:, 0:1], sq16[:, h * TH:(h + 1) * TH],
                                start=(h == 0), stop=(h == NCH - 1)),
                                incs=([(pe, 1)] if h == NCH - 1 else ()))
                        stats_mm = pe.v
                        A = astat
                        op("vector", lambda e, pst_a=pst_a: e.tensor_scalar(
                            A[0:1, 0:TH], pst_a[0:1, 0:TH], RD, None, ALU.mult),
                            waits=[(pe, stats_mm)])
                        op("vector", lambda e, pst_b=pst_b: e.tensor_scalar(
                            A[0:1, TH:2 * TH], pst_b[0:1, 0:TH], RD, None, ALU.mult))
                        op("vector", lambda e: e.tensor_tensor(
                            A[0:1, 2 * TH:3 * TH], A[0:1, 0:TH], A[0:1, 0:TH], ALU.mult))
                        op("vector", lambda e: e.scalar_tensor_tensor(
                            A[0:1, 3 * TH:4 * TH], A[0:1, TH:2 * TH], EPS,
                            A[0:1, 2 * TH:3 * TH], ALU.add, ALU.subtract),
                            incs=[(dv, 1)])
                        varB = dv.v
                        set_guard("pst", dv)
                        op("scalar", lambda e: e.activation(
                            A[0:1, 4 * TH:5 * TH], A[0:1, 3 * TH:4 * TH], AF.Ln, bias=0.0,
                            scale=1.0), waits=[(dv, varB)])
                        op("scalar", lambda e: e.activation(
                            A[0:1, 4 * TH:5 * TH], A[0:1, 4 * TH:5 * TH], AF.Exp, bias=0.0,
                            scale=-0.5), incs=[(ac, 1)])
                        rstdB = ac.v
                        op("vector", lambda e: e.scalar_tensor_tensor(
                            A[0:1, 5 * TH:6 * TH], A[0:1, 0:TH], -1.0, A[0:1, 4 * TH:5 * TH],
                            ALU.mult, ALU.mult), waits=[(ac, rstdB)], incs=[(dv, 1)])
                        nmrB = dv.v
                        op("tensor", lambda e, pbc=pbc: e.matmul(
                            pbc[:, 0:TH], ones32[0:1, :], A[0:1, 4 * TH:5 * TH],
                            start=True, stop=True), waits=[(dv, nmrB)] + gw("pbc"))
                        op("tensor", lambda e, pbc=pbc: e.matmul(
                            pbc[:, TH:2 * TH], ones32[0:1, :], A[0:1, 5 * TH:6 * TH],
                            start=True, stop=True), incs=[(pe, 1)])
                        bc_done = pe.v

                        for g4 in range(4):
                            lo, hi = g4 * 4 * TH, (g4 + 1) * 4 * TH
                            ws = [(pe, bc_done), (pe, stats_mm)] if g4 == 0 else ()
                            op("vector", lambda e, lo=lo, hi=hi, a16=a16, sq16=sq16,
                               pbc=pbc: e.tensor_tensor(
                                sq16[:, lo:hi], a16[:, lo:hi],
                                pbc[:, 0:TH].rearrange("p (a n) -> p a n", a=1)
                                .broadcast_to([P, 4, TH]), ALU.mult), waits=ws)
                            op("vector", lambda e, lo=lo, hi=hi, sq16=sq16,
                               pbc=pbc: e.tensor_tensor(
                                sq16[:, lo:hi], sq16[:, lo:hi],
                                pbc[:, TH:2 * TH].rearrange("p (a n) -> p a n", a=1)
                                .broadcast_to([P, 4, TH]), ALU.add))
                        for h in range(NCH):
                            op("vector", lambda e, h=h, sq16=sq16: e.tensor_scalar(
                                sq16[:, h * TH:(h + 1) * TH], sq16[:, h * TH:(h + 1) * TH],
                                lng_sb[:, h:h + 1], lnb_sb[:, h:h + 1], ALU.mult, ALU.add))
                        for g4 in range(4):
                            lo, hi = g4 * 4 * TH, (g4 + 1) * 4 * TH
                            op("vector", lambda e, lo=lo, hi=hi, a16=a16, sq16=sq16,
                               u16=u16: e.tensor_tensor(
                                a16[:, lo:hi], sq16[:, lo:hi], u16[:, lo:hi], ALU.mult),
                                incs=[(dv, 1)])
                        apply_dv = dv.v
                        set_guard("pbc", dv)
                        set_guard("u16", dv)

                        for fo in range(NCH):
                            slot = nslice % 2
                            nslice += 1
                            op("gpsimd", lambda e, slot=slot, fo=fo, wu_sl=wu_sl: e.dma_start(
                                out=wu_sl[slot][:], in_=din["wo_t"][fo]),
                                waits=gw(("wusl", slot)), incs=[(swu[slot], 16)])
                            wld = swu[slot].v
                            for kc in range(NCH):
                                ws = []
                                if kc == 0:
                                    ws = [(swu[slot], wld), (dv, apply_dv)] + gw(("pu", fo % 2))
                                op("tensor", lambda e, slot=slot, fo=fo, kc=kc, wu_sl=wu_sl,
                                   pu=pu, a16=a16: e.matmul(
                                    pu[fo % 2][:], wu_sl[slot][:, kc * P:(kc + 1) * P],
                                    a16[:, kc * TH:(kc + 1) * TH],
                                    start=(kc == 0), stop=(kc == NCH - 1)),
                                    waits=ws, incs=([(pe, 1)] if kc == NCH - 1 else ()))
                            set_guard(("wusl", slot), pe)
                            omm = pe.v
                            op("vector", lambda e, fo=fo, pu=pu, outf=outf: e.tensor_scalar(
                                outf[:, fo * TH:(fo + 1) * TH], pu[fo % 2][:],
                                bo_sb[:, fo:fo + 1], None, ALU.add),
                                waits=[(pe, omm)], incs=[(dv, 1)])
                            set_guard(("pu", fo % 2), dv)
                        wo_dv = dv.v

                    with ExitStack() as pctx2:
                        ptr = pctx2.enter_context(nc.psum_tensor(f"ptr_{st}_{th}", [P, D], F32))
                        h2tr = pctx2.enter_context(nc.psum_tensor(f"h2tr_{st}_{th}", [P, D], BF16))
                        for tsl in range(2):
                            ts_g = th * 2 + tsl
                            xb = ts_g % 2
                            row0 = (st * NTS + ts_g) * P
                            op("sync", lambda e, row0=row0, xb=xb: e.dma_start(
                                out=xB[xb][:], in_=din["x_own"][row0:row0 + P, :]),
                                waits=gw(("xB", xb)), incs=[(sxb[xb], 16)])
                            xbld = sxb[xb].v
                            for fo in range(NCH):
                                ws = ([(dv, wo_dv)] + gw("ptr")) if fo == 0 else ()
                                op("tensor", lambda e, fo=fo, tsl=tsl, ptr=ptr, outf=outf:
                                   e.transpose(
                                    ptr[:, fo * P:(fo + 1) * P],
                                    outf[:, fo * TH + tsl * P: fo * TH + tsl * P + P],
                                    id32_sb[:]),
                                    waits=ws, incs=([(pe, 1)] if fo == NCH - 1 else ()))
                            tr_done = pe.v
                            op("vector", lambda e, ts_g=ts_g, xb=xb, ptr=ptr: e.tensor_tensor(
                                ACC[:, ts_g * D:(ts_g + 1) * D], ptr[:], xB[xb][:], ALU.add),
                                waits=[(pe, tr_done), (sxb[xb], xbld)] + gw(("ACC", ts_g)),
                                incs=[(dv, 1)])
                            set_guard(("xB", xb), dv)
                            set_guard("ptr", dv)
                            acc_dv = dv.v

                            op("scalar", lambda e, ts_g=ts_g: e.activation(
                                h2t16[:], ACC[:, ts_g * D:(ts_g + 1) * D], AF.Copy,
                                bias=0.0, scale=1.0, accum_out=stat2[:, 0:1]),
                                waits=[(dv, acc_dv)] + gw("stat2a") + gw("h2t16"), incs=[(ac, 1)])
                            op("scalar", lambda e, ts_g=ts_g: e.activation(
                                h2t16[:], ACC[:, ts_g * D:(ts_g + 1) * D], AF.Square,
                                bias=0.0, scale=1.0, accum_out=stat2[:, 1:2]), incs=[(ac, 1)])
                            sq2 = ac.v
                            op("vector", lambda e: e.tensor_scalar(
                                stat2[:, 2:3], stat2[:, 0:1], RD, None, ALU.mult),
                                waits=[(ac, sq2)])
                            op("vector", lambda e: e.tensor_scalar(
                                stat2[:, 3:4], stat2[:, 1:2], RD, None, ALU.mult),
                                waits=[(ac, sq2)])
                            op("vector", lambda e: e.tensor_tensor(
                                stat2[:, 4:5], stat2[:, 2:3], stat2[:, 2:3], ALU.mult))
                            op("vector", lambda e: e.scalar_tensor_tensor(
                                stat2[:, 5:6], stat2[:, 3:4], EPS, stat2[:, 4:5],
                                ALU.add, ALU.subtract), incs=[(dv, 1)])
                            var2 = dv.v
                            op("scalar", lambda e: e.activation(
                                stat2[:, 6:7], stat2[:, 6:7] if False else stat2[:, 5:6],
                                AF.Ln, bias=0.0, scale=1.0),
                                waits=[(dv, var2)])
                            op("scalar", lambda e: e.activation(
                                stat2[:, 6:7], stat2[:, 6:7], AF.Exp, bias=0.0, scale=-0.5),
                                incs=[(ac, 1)])
                            rstd2 = ac.v
                            op("vector", lambda e: e.scalar_tensor_tensor(
                                stat2[:, 7:8], stat2[:, 2:3], -1.0, stat2[:, 6:7],
                                ALU.mult, ALU.mult), waits=[(ac, rstd2)])
                            op("vector", lambda e, ts_g=ts_g: e.tensor_scalar(
                                h2t16[:], ACC[:, ts_g * D:(ts_g + 1) * D],
                                stat2[:, 6:7], stat2[:, 7:8], ALU.mult, ALU.add),
                                incs=[(dv, 1)])
                            set_guard("stat2a", dv)
                            h2t_dv = dv.v
                            for ch in range(NCH):
                                ws = ([(dv, h2t_dv)] + gw("h2tr")) if ch == 0 else ()
                                op("tensor", lambda e, ch=ch, h2tr=h2tr: e.transpose(
                                    h2tr[:, ch * P:(ch + 1) * P], h2t16[:, ch * P:(ch + 1) * P],
                                    id16_sb[:]),
                                    waits=ws, incs=([(pe, 1)] if ch == NCH - 1 else ()))
                            h2tr_pe = pe.v
                            set_guard("h2t16", pe)
                            op("vector", lambda e, ts_g=ts_g, h2tr=h2tr: e.tensor_copy(
                                h2f[:, ts_g * D:(ts_g + 1) * D], h2tr[:]),
                                waits=[(pe, h2tr_pe)] + gw(("h2f", ts_g)), incs=[(dv, 1)])
                            set_guard("h2tr", dv)
                            set_guard(("h2f_rdy", ts_g), dv)

                    for k in (("pa", 0), ("pa", 1), ("pu", 0), ("pu", 1), "pst", "pbc",
                              "ptr", "h2tr"):
                        set_guard(k, dv)

                sctx.close()
                if stage == 2:
                    for ts8 in range(NTS):
                        row0 = (st * NTS + ts8) * P
                        op("sync", lambda e, ts8=ts8, row0=row0: e.dma_start(
                            out=o[row0:row0 + P, :], in_=ACC[:, ts8 * D:(ts8 + 1) * D]),
                            waits=[(dv, dv.v)], incs=[(sout, 16)])
                        set_guard(("ACC", ts8), sout)
                    for ts_g in range(NTS):
                        set_guard(("h2f", ts_g), dv)
                    continue
                ffn_entry = [(pe, pe.v), (dv, dv.v), (ac, ac.v)]
                first_gp_f = [True]

                def fgp():
                    if first_gp_f[0]:
                        first_gp_f[0] = False
                        return list(ffn_entry)
                    return []

                with ExitStack() as fctx:
                    gated = fctx.enter_context(nc.sbuf_tensor(f"gated{st}", [P, GCH * 1024], BF16))
                    wgu_sl = [fctx.enter_context(
                        nc.sbuf_tensor(f"wgu_sl{st}_{i}", [P, 2 * NCH * P], BF16))
                        for i in range(2)]
                    wd_sb = [fctx.enter_context(
                        nc.sbuf_tensor(f"wd_sb{st}_0", [P, 24 * 512], BF16))]
                    silu_scr = fctx.enter_context(nc.sbuf_tensor(f"silu_scr{st}", [P, 512], BF16))
                    pg = [fctx.enter_context(nc.psum_tensor(f"pg_{st}_{i}", [P, 512], F32))
                          for i in range(2)]
                    pu2 = [fctx.enter_context(nc.psum_tensor(f"pu2_{st}_{i}", [P, 512], F32))
                           for i in range(2)]
                    pd = [fctx.enter_context(nc.psum_tensor(f"pd_{st}_{i}", [P, 512], F32))
                          for i in range(3)]
                    prt_f = fctx.enter_context(nc.psum_tensor(f"prt_{st}", [P, 512], F32))
                    prt = prt_f[:, 0:32]
                    for k in (("pg", 0), ("pg", 1), ("pu2", 0), ("pu2", 1),
                              ("pd", 0), ("pd", 1), ("pd", 2), "prt"):
                        set_guard(k, dv)

                    for ts_g in range(NTS):
                        for kc in range(NCH):
                            ws = (gw(("h2f_rdy", ts_g)) + gw("prt")) if kc == 0 else ()
                            op("tensor", lambda e, ts_g=ts_g, kc=kc, prt=prt: e.matmul(
                                prt, h2f[:, ts_g * D + kc * P: ts_g * D + (kc + 1) * P],
                                gate16[:, kc * 32:(kc + 1) * 32],
                                start=(kc == 0), stop=(kc == NCH - 1)),
                                waits=ws, incs=([(pe, 1)] if kc == NCH - 1 else ()))
                        rmm = pe.v
                        op("vector", lambda e, prt=prt: e.reduce_max(
                            msc[:, 0:1], prt, axis=AX.X),
                            waits=[(pe, rmm)] + gw("router_bufs"), incs=[(dv, 1)])
                        rmax1 = dv.v
                        op("vector", lambda e: e.tensor_scalar(
                            msc[:, 1:2], msc[:, 0:1], -1.0, None, ALU.mult),
                            waits=[(dv, rmax1)], incs=[(dv, 1)])
                        nm1 = dv.v
                        op("scalar", lambda e, prt=prt: e.activation(
                            rw_sb[:], prt, AF.Exp, bias=msc[:, 1:2], scale=1.0),
                            waits=[(dv, nm1)], incs=[(ac, 1)])
                        set_guard("prt", ac)
                        expd = ac.v
                        op("vector", lambda e: e.tensor_copy(rwm_sb[:], rw_sb[:]),
                           waits=[(ac, expd)], incs=[(dv, 1)])
                        for _ in range(3):
                            wcp = dv.v
                            op("vector", lambda e: e.reduce_max(msc[:, 2:3], rwm_sb[:], axis=AX.X),
                               waits=[(dv, wcp)], incs=[(dv, 1)])
                            rmx = dv.v
                            op("vector", lambda e: e.tensor_single_scalar(
                                mk_sb[:], rwm_sb[:], msc[:, 2:3], ALU.is_ge),
                                waits=[(dv, rmx)])
                            op("vector", lambda e: e.scalar_tensor_tensor(
                                rwm_sb[:], mk_sb[:], -1e30, rwm_sb[:], ALU.mult, ALU.add),
                                incs=[(dv, 1)])
                        wcp4 = dv.v
                        op("vector", lambda e: e.reduce_max(msc[:, 2:3], rwm_sb[:], axis=AX.X),
                           waits=[(dv, wcp4)], incs=[(dv, 1)])
                        rmx4 = dv.v
                        op("vector", lambda e: e.tensor_single_scalar(
                            mk_sb[:], rw_sb[:], msc[:, 2:3], ALU.is_ge),
                            waits=[(dv, rmx4)])
                        op("vector", lambda e: e.tensor_tensor(
                            rwm_sb[:], rw_sb[:], mk_sb[:], ALU.mult), incs=[(dv, 1)])
                        wwr = dv.v
                        op("vector", lambda e: e.reduce_sum(msc[:, 3:4], rwm_sb[:], axis=AX.X),
                           waits=[(dv, wwr)], incs=[(dv, 1)])
                        rs4 = dv.v
                        op("vector", lambda e: e.reciprocal(msc[:, 4:5], msc[:, 3:4]),
                           waits=[(dv, rs4)], incs=[(dv, 1)])
                        rrec = dv.v
                        op("vector", lambda e, ts_g=ts_g: e.tensor_scalar(
                            w_sb[:, ts_g * 32:(ts_g + 1) * 32], rwm_sb[:], msc[:, 4:5], None,
                            ALU.mult), waits=[(dv, rrec)], incs=[(dv, 1)])
                    set_guard("router_bufs", dv)
                    w_done = dv.v

                    ev_done = {}
                    for ex in range(NE):
                        dslot = 0
                        op("gpsimd", lambda e, ex=ex, dslot=dslot, wd_sb=wd_sb: e.dma_start(
                            out=wd_sb[dslot][:], in_=din["wd_t"][ex]),
                            waits=fgp() + gw(("wd", dslot)), incs=[(swd[dslot], 16)])
                        wdld = swd[dslot].v

                        for mcp in range(GCH):
                            gslot = ngup % 2
                            ngup += 1
                            op("gpsimd", lambda e, ex=ex, mcp=mcp, gslot=gslot,
                               wgu_sl=wgu_sl: e.dma_start(
                                out=wgu_sl[gslot][:, 0:NCH * P], in_=din["gup_t"][ex, mcp]),
                                waits=gw(("wgu", gslot)), incs=[(swg[gslot], 16)])
                            op("gpsimd", lambda e, ex=ex, mcp=mcp, gslot=gslot,
                               wgu_sl=wgu_sl: e.dma_start(
                                out=wgu_sl[gslot][:, NCH * P:2 * NCH * P],
                                in_=din["gup_t"][ex, mcp + GCH]),
                                incs=[(swg[gslot], 16)])
                            gld = swg[gslot].v
                            for th2 in range(2):
                                ps_i = (mcp * 2 + th2) % 2
                                for kc in range(NCH):
                                    ws = []
                                    if kc == 0:
                                        ws = [(swg[gslot], gld)] + gw(("pg", ps_i))
                                        if ex == 0 and mcp == 0 and th2 == 0:
                                            ws += [g for t2 in range(NTS)
                                                   for g in gw(("h2f_rdy", t2))]
                                    op("tensor", lambda e, gslot=gslot, ps_i=ps_i, kc=kc,
                                       th2=th2, wgu_sl=wgu_sl, pg=pg: e.matmul(
                                        pg[ps_i][:], wgu_sl[gslot][:, kc * P:(kc + 1) * P],
                                        h2f.rearrange("p (t c j) -> p t c j", t=NTS, c=NCH)[
                                            :, th2 * 4:(th2 + 1) * 4, kc, :],
                                        start=(kc == 0), stop=(kc == NCH - 1)), waits=ws)
                                for kc in range(NCH):
                                    ws = gw(("pu2", ps_i)) if kc == 0 else ()
                                    op("tensor", lambda e, gslot=gslot, ps_i=ps_i, kc=kc,
                                       th2=th2, wgu_sl=wgu_sl, pu2=pu2: e.matmul(
                                        pu2[ps_i][:],
                                        wgu_sl[gslot][:, NCH * P + kc * P: NCH * P + (kc + 1) * P],
                                        h2f.rearrange("p (t c j) -> p t c j", t=NTS, c=NCH)[
                                            :, th2 * 4:(th2 + 1) * 4, kc, :],
                                        start=(kc == 0), stop=(kc == NCH - 1)),
                                        waits=ws, incs=([(pe, 1)] if kc == NCH - 1 else ()))
                                gumm = pe.v
                                set_guard(("wgu", gslot), pe)
                                op("scalar", lambda e, ps_i=ps_i, pg=pg,
                                   silu_scr=silu_scr: e.activation(
                                    silu_scr[:], pg[ps_i][:], AF.Silu, bias=0.0, scale=1.0),
                                    waits=[(pe, gumm)] + gw("silu_scr"), incs=[(ac, 1)])
                                set_guard(("pg", ps_i), ac)
                                sdone = ac.v
                                op("vector", lambda e, ps_i=ps_i, mcp=mcp, th2=th2, pu2=pu2,
                                   gated=gated, silu_scr=silu_scr: e.tensor_tensor(
                                    gated[:, mcp * 1024 + th2 * 512: mcp * 1024 + (th2 + 1) * 512],
                                    pu2[ps_i][:], silu_scr[:], ALU.mult),
                                    waits=[(ac, sdone)], incs=[(dv, 1)])
                                set_guard(("pu2", ps_i), dv)
                                set_guard("silu_scr", dv)
                        gated_dv = dv.v

                        for ts8 in range(NTS):
                            th2 = ts8 // 4
                            j0 = (ts8 % 4) * P
                            for no in range(4):
                                ds = (ts8 * 4 + no) % 3
                                for kc in range(GCH):
                                    ws = []
                                    if kc == 0:
                                        ws = ([(swd[dslot], wdld), (dv, gated_dv)]
                                              + gw(("pd", ds)))
                                    op("tensor", lambda e, ds=ds, kc=kc, no=no, th2=th2,
                                       j0=j0, dslot=dslot, gated=gated, wd_sb=wd_sb,
                                       pd=pd: e.matmul(
                                        pd[ds][:],
                                        gated[:, kc * 1024 + th2 * 512 + j0:
                                              kc * 1024 + th2 * 512 + j0 + P],
                                        wd_sb[dslot][:, (no * GCH + kc) * 512:
                                                     (no * GCH + kc + 1) * 512],
                                        start=(kc == 0), stop=(kc == GCH - 1)),
                                        waits=ws, incs=([(pe, 1)] if kc == GCH - 1 else ()))
                                dmm = pe.v
                                wsx = [(pe, dmm)]
                                if ex == 0 and no == 0:
                                    wsx.append((dv, w_done))
                                op("vector", lambda e, ds=ds, ts8=ts8, no=no, ex=ex,
                                   pd=pd: e.scalar_tensor_tensor(
                                    ACC[:, ts8 * D + no * 512: ts8 * D + (no + 1) * 512],
                                    pd[ds][:], w_sb[:, ts8 * 32 + ex: ts8 * 32 + ex + 1],
                                    ACC[:, ts8 * D + no * 512: ts8 * D + (no + 1) * 512],
                                    ALU.mult, ALU.add),
                                    waits=wsx, incs=[(dv, 1)])
                                set_guard(("pd", ds), dv)
                                if ex == NE - 1 and no == 3:
                                    ev_done[ts8] = dv.v
                        set_guard(("wd", dslot), pe)

                    for ts8 in range(NTS):
                        row0 = (st * NTS + ts8) * P
                        op("sync", lambda e, ts8=ts8, row0=row0: e.dma_start(
                            out=o[row0:row0 + P, :], in_=ACC[:, ts8 * D:(ts8 + 1) * D]),
                            waits=[(dv, ev_done[ts8])], incs=[(sout, 16)])
                        set_guard(("ACC", ts8), sout)
                    for ts_g in range(NTS):
                        set_guard(("h2f", ts_g), pe)

            op("sync", lambda e: e.nop(), waits=[(sout, sout.v)])

        build_nc._last_plan = plan
        build_nc._last_nc = nc
        fl_v = Sem(ctx, nc, "fl_v")
        fl_a = Sem(ctx, nc, "fl_a")
        fl_p = Sem(ctx, nc, "fl_p")
        flush_sem = {"vector": fl_v, "scalar": fl_a, "pool": fl_p}
        with nc.Block() as block:
            def runner(eng_name):
                fs = flush_sem.get(eng_name)
                names = {eng_name, "pool"} if eng_name == "gpsimd" else {eng_name}
                def f(eng):
                    for e_, fn, ws, incs in plan:
                        if e_ not in names:
                            continue
                        if e_ == "pool":
                            fs2 = flush_sem["pool"]
                        else:
                            fs2 = fs if e_ != "gpsimd" else None
                        for s, v in ws:
                            eng.wait_ge(s.h, v)
                        inst = fn(eng)
                        if fs2 is not None:
                            fs2.v += 1
                            inst.then_inc(fs2.h, 1)
                            eng.wait_ge(fs2.h, fs2.v)
                            if incs:
                                inst = eng.nop()
                        for s, k in incs:
                            inst.then_inc(s.h, k)
                return f
            block.sync(runner("sync"))
            block.gpsimd(runner("gpsimd"))
            block.tensor(runner("tensor"))
            block.vector(runner("vector"))
            block.scalar(runner("scalar"))
    return nc


_NC_CACHE = {}


def _get_nc():
    if "nc" not in _NC_CACHE:
        _NC_CACHE["nc"] = build_nc()
    return _NC_CACHE["nc"]


def kernel(L0_attn_x, L0_attn_Wqk, L0_attn_bqk, L0_attn_Wv, L0_attn_bv,
           L0_attn_Wu, L0_attn_bu, L0_attn_Wo, L0_attn_bo, L0_attn_ln_g,
           L0_attn_ln_b, L0_ffn_gate, L0_ffn_gate_up_proj, L0_ffn_down_proj):
    c = np.ascontiguousarray
    x = np.asarray(L0_attn_x).reshape(8 * 4096, D)
    wqk = c(np.asarray(L0_attn_Wqk))
    wv = c(np.asarray(L0_attn_Wv))
    wu_t = c(np.asarray(L0_attn_Wu).reshape(NCH, P, NCH, P).transpose(2, 1, 0, 3))
    wo_t = c(np.asarray(L0_attn_Wo).reshape(NCH, P, NCH, P).transpose(2, 1, 0, 3))
    gup_t = c(np.asarray(L0_ffn_gate_up_proj).reshape(NE, NCH, P, 12, P)
              .transpose(0, 3, 2, 1, 4))
    wd_t = c(np.asarray(L0_ffn_down_proj).reshape(NE, GCH, P, 4, 512)
             .transpose(0, 2, 3, 1, 4))
    gate_t = c(np.asarray(L0_ffn_gate).reshape(NCH, P, 32).transpose(1, 0, 2))
    bqk_r = c(np.asarray(L0_attn_bqk).reshape(1, D))
    bv_r = c(np.asarray(L0_attn_bv).reshape(1, D))
    bo_c = c(np.asarray(L0_attn_bo).reshape(NCH, P).T)
    bu_c = c(np.asarray(L0_attn_bu).reshape(NCH, P).T)
    lng_c = c(np.asarray(L0_attn_ln_g).reshape(NCH, P).T)
    lnb_c = c(np.asarray(L0_attn_ln_b).reshape(NCH, P).T)
    ones_r16 = np.ones((1, P), dtype=ml_dtypes.bfloat16)
    ones_c16 = np.ones((P, 1), dtype=ml_dtypes.bfloat16)
    ones_r32 = np.ones((1, P), dtype=np.float32)
    id16 = np.eye(P, dtype=ml_dtypes.bfloat16)
    id32 = np.eye(P, dtype=np.float32)

    shared = dict(wqk=wqk, wv=wv, wu_t=wu_t, wo_t=wo_t, gup_t=gup_t, wd_t=wd_t,
                  gate_t=gate_t, bqk_r=bqk_r, bv_r=bv_r, bo_c=bo_c, bu_c=bu_c,
                  lng_c=lng_c, lnb_c=lnb_c, ones_r16=ones_r16, ones_c16=ones_c16,
                  ones_r32=ones_r32, id16=id16, id32=id32)
    in_maps = []
    for core in range(8):
        m = dict(shared)
        m["x_own"] = c(x[core * 4096:(core + 1) * 4096])
        in_maps.append(m)

    nc = _get_nc()
    res = run_bass_kernel_spmd(nc, in_maps, core_ids=list(range(8)))
    out = np.concatenate([np.asarray(res.results[core]["o"]) for core in range(8)], axis=0)
    return out.reshape(4, 8192, D).astype(np.float32, copy=False)


# version 07
# speedup vs baseline: 1.1516x; 1.1516x over previous
import numpy as np
import ml_dtypes
from contextlib import ExitStack

import concourse.bass as bass
import concourse.mybir as mybir
from concourse.bass_utils import run_bass_kernel_spmd

F32 = mybir.dt.float32
BF16 = mybir.dt.bfloat16
AF = mybir.ActivationFunctionType
ALU = mybir.AluOpType
AX = mybir.AxisListType

P = 128
D = 2048
NCH = 16
NT = 32
NTA = 64
NST = 4
NTH = 4
TH = 256
NTS = 8
NE = 32
GCH = 6
EPS = 1e-5
RD = 1.0 / D


class Sem:
    def __init__(self, ctx, nc, name):
        self.h = ctx.enter_context(nc.semaphore(name))
        self.v = 0


def build_nc(stage=3):
    nc = bass.Bass("TRN2", target_bir_lowering=False, debug=False)
    dt_in = {
        "x_own": ([4096, D], F32),
        "wqk": ([D, D], F32), "wv": ([D, D], F32),
        "wu_t": ([NCH, P, NCH, P], F32), "wo_t": ([NCH, P, NCH, P], F32),
        "gup_t": ([NE, 12, P, NCH, P], F32), "wd_t": ([NE, P, 4, GCH, 512], F32),
        "gate_t": ([P, NCH, 32], F32),
        "bqk_r": ([1, D], F32), "bv_r": ([1, D], F32), "bo_c": ([P, NCH], F32),
        "bu_c": ([P, NCH], F32), "lng_c": ([P, NCH], F32), "lnb_c": ([P, NCH], F32),
        "ones_r16": ([1, P], BF16), "ones_c16": ([P, 1], BF16), "ones_r32": ([1, P], F32),
        "id16": ([P, P], BF16), "id32": ([P, P], F32),
    }
    din = {k: nc.dram_tensor(k, s, d, kind="ExternalInput").ap() for k, (s, d) in dt_in.items()}
    o = nc.dram_tensor("o", [4096, D], F32, kind="ExternalOutput").ap()
    hf_s = nc.dram_tensor("hf_s", [NT, P, D], BF16, kind="Internal").ap()
    kv_cc_in = nc.dram_tensor("kv_cc_in", [P, D], F32, kind="Internal").ap()
    kv_cc_out = nc.dram_tensor("kv_cc_out", [P, D], F32, kind="Internal").ap()
    qf_s = nc.dram_tensor("qf_s", [NT, P, D], BF16, kind="Internal").ap()

    plan = []

    def op(eng, fn, waits=(), incs=()):
        seen = {}
        for s, v in waits:
            if v > 0 and (s not in seen or seen[s] < v):
                seen[s] = v
        for s, k in incs:
            s.v += k
        plan.append((eng, fn, list(seen.items()), list(incs)))

    guards = {}

    def set_guard(key, *sems):
        guards[key] = [(s, s.v) for s in sems]

    def gw(*keys):
        out = []
        for k in keys:
            out.extend(guards.get(k, ()))
        return out

    ctx = ExitStack()
    with ctx:
        sy = Sem(ctx, nc, "sy")
        sst = Sem(ctx, nc, "sst")
        sout = Sem(ctx, nc, "sout")
        gp = Sem(ctx, nc, "gp")
        pe = Sem(ctx, nc, "pe")
        po = Sem(ctx, nc, "po")
        dv = Sem(ctx, nc, "dv")
        ac = Sem(ctx, nc, "ac")
        sx = [Sem(ctx, nc, f"sx{i}") for i in range(2)]
        shfq = Sem(ctx, nc, "shfq")
        sxb = [Sem(ctx, nc, f"sxb{i}") for i in range(2)]
        swu = [Sem(ctx, nc, f"swu{i}") for i in range(2)]
        swg = [Sem(ctx, nc, f"swg{i}") for i in range(2)]
        swd = [Sem(ctx, nc, f"swd{i}") for i in range(2)]

        kv16 = ctx.enter_context(nc.sbuf_tensor("kv16", [P, D], BF16))
        bqk16 = ctx.enter_context(nc.sbuf_tensor("bqk16", [1, D], BF16))
        bv16 = ctx.enter_context(nc.sbuf_tensor("bv16", [1, D], BF16))
        bu_sb = ctx.enter_context(nc.sbuf_tensor("bu_sb", [P, NCH], F32))
        bo_sb = ctx.enter_context(nc.sbuf_tensor("bo_sb", [P, NCH], F32))
        lng_sb = ctx.enter_context(nc.sbuf_tensor("lng_sb", [P, NCH], F32))
        lnb_sb = ctx.enter_context(nc.sbuf_tensor("lnb_sb", [P, NCH], F32))
        ones_r = ctx.enter_context(nc.sbuf_tensor("ones_r", [1, P], BF16))
        ones_c = ctx.enter_context(nc.sbuf_tensor("ones_c", [P, 1], BF16))
        ones32 = ctx.enter_context(nc.sbuf_tensor("ones32", [1, P], F32))
        id16_sb = ctx.enter_context(nc.sbuf_tensor("id16_sb", [P, P], BF16))
        id32_sb = ctx.enter_context(nc.sbuf_tensor("id32_sb", [P, P], F32))
        gate16 = ctx.enter_context(nc.sbuf_tensor("gate16", [P, NCH * 32], BF16))

        for dst, key in [(bu_sb, "bu_c"), (bo_sb, "bo_c"), (lng_sb, "lng_c"),
                         (lnb_sb, "lnb_c"), (ones_r, "ones_r16"), (ones_c, "ones_c16"),
                         (ones32, "ones_r32"), (id16_sb, "id16"), (id32_sb, "id32")]:
            op("sync", lambda e, d=dst, k=key: e.dma_start(out=d[:], in_=din[k]),
               incs=[(sy, 16)])
        for dst, key in [(bqk16, "bqk_r"), (bv16, "bv_r"), (gate16, "gate_t")]:
            op("gpsimd", lambda e, d=dst, k=key: e.dma_start(out=d[:], in_=din[k]),
               incs=[(gp, 16)])
        misc_sy = sy.v

        with ExitStack() as actx:
            warm16 = actx.enter_context(nc.sbuf_tensor("warm16", [P, 512], BF16))
            wqk_sb = actx.enter_context(nc.sbuf_tensor("wqk_sb", [P, NCH * D], BF16))
            wv_sb = actx.enter_context(nc.sbuf_tensor("wv_sb", [P, NCH * D], BF16))
            x_t = [actx.enter_context(nc.sbuf_tensor(f"x_t{i}", [P, D], F32)) for i in range(2)]
            stat = [actx.enter_context(nc.sbuf_tensor(f"stat{i}", [P, 8], F32)) for i in range(2)]
            h16 = [actx.enter_context(nc.sbuf_tensor(f"h16_{i}", [P, D], BF16)) for i in range(2)]
            hf_sb = [actx.enter_context(nc.sbuf_tensor(f"hf_sb{i}", [P, D], BF16)) for i in range(2)]
            q16 = [actx.enter_context(nc.sbuf_tensor(f"q16_{i}", [P, D], BF16)) for i in range(2)]
            v16 = [actx.enter_context(nc.sbuf_tensor("v16_0", [P, D], BF16))]
            kv_acc = actx.enter_context(nc.sbuf_tensor("kv_acc", [P, D], F32))
            qf_sb = [actx.enter_context(nc.sbuf_tensor("qf_sb0", [P, D], BF16))] * 2
            r_sb = [actx.enter_context(nc.sbuf_tensor("r_sb0", [P, 512], F32))]
            m_sb = [actx.enter_context(nc.sbuf_tensor("m_sb0", [P, 512], F32))]
            e_sb = [actx.enter_context(nc.sbuf_tensor("e_sb0", [P, 512], F32))]
            kv_ps = actx.enter_context(nc.psum_tensor("kv_ps", [P, D], F32))
            htr_ps = actx.enter_context(nc.psum_tensor("htr_ps", [P, D], BF16))
            qv_ps = [actx.enter_context(nc.psum_tensor(f"qv_ps{i}", [P, 512], F32)) for i in range(2)]

            for kc in range(NCH):
                op("gpsimd", lambda e, kc=kc: e.dma_start(
                    out=wqk_sb[:, kc * D:(kc + 1) * D], in_=din["wqk"][kc * P:(kc + 1) * P, :]),
                    incs=[(gp, 16)])
            for kc in range(NCH):
                op("gpsimd", lambda e, kc=kc: e.dma_start(
                    out=wv_sb[:, kc * D:(kc + 1) * D], in_=din["wv"][kc * P:(kc + 1) * P, :]),
                    incs=[(gp, 16)])
            wA_gp = gp.v

            for wi in range(40):
                op("tensor", lambda e: e.matmul(
                    qv_ps[0][:], warm16[:, 0:P], warm16[:], start=True, stop=True))

            nqv = 0
            tinfo = {}
            for tt_i in range(NT + 1):
              for part, t in ((1, tt_i), (2, tt_i - 1)):
                if part == 1 and t >= NT:
                    continue
                if part == 2 and t < 0:
                    continue
                b = t % 2
                own = True
                src = din["x_own"]
                row0 = t * P
                if part == 1:

                    op("sync", lambda e, s=src, r=row0, b=b: e.dma_start(
                        out=x_t[b][:], in_=s[r:r + P, :]),
                        waits=gw(("x", b)), incs=[(sx[b], 16)])
                    xw = [(sx[b], sx[b].v)]

                    pro = [(sy, misc_sy)] if t == 0 else []
                    op("scalar", lambda e, b=b: e.activation(
                        h16[b][:], x_t[b][:], AF.Copy, bias=0.0, scale=1.0,
                        accum_out=stat[b][:, 0:1]),
                        waits=xw + pro + gw(("stat", b)) + gw(("h16", b)), incs=[(ac, 1)])
                    op("scalar", lambda e, b=b: e.activation(
                        h16[b][:], x_t[b][:], AF.Square, bias=0.0, scale=1.0,
                        accum_out=stat[b][:, 1:2]), incs=[(ac, 1)])
                    sq_done = ac.v
                    op("vector", lambda e, b=b: e.tensor_scalar(
                        stat[b][:, 2:3], stat[b][:, 0:1], RD, None, ALU.mult),
                        waits=[(ac, sq_done)])
                    op("vector", lambda e, b=b: e.tensor_scalar(
                        stat[b][:, 3:4], stat[b][:, 1:2], RD, None, ALU.mult),
                        waits=[(ac, sq_done)])
                    op("vector", lambda e, b=b: e.tensor_tensor(
                        stat[b][:, 4:5], stat[b][:, 2:3], stat[b][:, 2:3], ALU.mult))
                    op("vector", lambda e, b=b: e.scalar_tensor_tensor(
                        stat[b][:, 5:6], stat[b][:, 3:4], EPS, stat[b][:, 4:5],
                        ALU.add, ALU.subtract), incs=[(dv, 1)])
                    var_done = dv.v
                    op("scalar", lambda e, b=b: e.activation(
                        stat[b][:, 6:7], stat[b][:, 5:6], AF.Ln, bias=0.0, scale=1.0),
                        waits=[(dv, var_done)])
                    op("scalar", lambda e, b=b: e.activation(
                        stat[b][:, 6:7], stat[b][:, 6:7], AF.Exp, bias=0.0, scale=-0.5),
                        incs=[(ac, 1)])
                    rstd_done = ac.v
                    op("vector", lambda e, b=b: e.scalar_tensor_tensor(
                        stat[b][:, 7:8], stat[b][:, 2:3], -1.0, stat[b][:, 6:7], ALU.mult, ALU.mult),
                        waits=[(ac, rstd_done)])
                    op("vector", lambda e, b=b: e.tensor_scalar(
                        h16[b][:], x_t[b][:], stat[b][:, 6:7], stat[b][:, 7:8], ALU.mult, ALU.add),
                        incs=[(dv, 1)])
                    apply_done = dv.v
                    set_guard(("x", b), dv, ac)
                    set_guard(("stat", b), dv)

                    for ch in range(NCH):
                        op("tensor", lambda e, b=b, ch=ch: e.transpose(
                            htr_ps[:, ch * P:(ch + 1) * P], h16[b][:, ch * P:(ch + 1) * P], id16_sb[:]),
                            waits=([(dv, apply_done)] + pro + gw("htr")) if ch == 0 else (),
                            incs=([(pe, 1)] if ch == NCH - 1 else ()))
                    htr_done = pe.v
                    set_guard(("h16", b), pe)
                    op("vector", lambda e, b=b: e.tensor_copy(hf_sb[b][:], htr_ps[:]),
                       waits=[(pe, htr_done)] + gw(("hf", b)), incs=[(dv, 1)])
                    set_guard("htr", dv)
                    hf_done = dv.v
                    if own:
                        op("sync", lambda e, t=t, b=b: e.dma_start(out=hf_s[t], in_=hf_sb[b][:]),
                           waits=[(dv, hf_done)], incs=[(sst, 16)])
                    hf_store_v = sst.v

                    tinfo[t] = dict(hf_done=hf_done, apply_done=apply_done)
                if part == 2:
                    hf_done = tinfo[t]['hf_done']
                    bv_ = 0
                    for which, wsb, brow in (("q", wqk_sb, bqk16), ("v", wv_sb, bv16)):
                        for no in range(4):
                            slot = nqv % 2
                            nqv += 1
                            ws = [(dv, hf_done)] + gw(("qv", slot))
                            if t == 0:
                                ws.append((gp, wA_gp))
                            op("tensor", lambda e, slot=slot, brow=brow, no=no: e.matmul(
                                qv_ps[slot][:], ones_r[0:1, :], brow[0:1, no * 512:(no + 1) * 512],
                                start=True, stop=False), waits=ws)
                            for kc in range(NCH):
                                op("tensor", lambda e, slot=slot, b=b, kc=kc, wsb=wsb, no=no: e.matmul(
                                    qv_ps[slot][:], hf_sb[b][:, kc * P:(kc + 1) * P],
                                    wsb[:, kc * D + no * 512: kc * D + (no + 1) * 512],
                                    start=False, stop=(kc == NCH - 1)),
                                    incs=([(pe, 1)] if kc == NCH - 1 else ()))
                            mm_done = pe.v
                            if which == "q":
                                e2 = 0
                                op("scalar", lambda e, slot=slot, e2=e2: e.activation(
                                    r_sb[e2][:], qv_ps[slot][:], AF.Relu, bias=0.0, scale=1.0),
                                    waits=[(pe, mm_done)] + gw(("rm", e2)))
                                op("scalar", lambda e, slot=slot, e2=e2: e.activation(
                                    m_sb[e2][:], qv_ps[slot][:], AF.Relu, bias=0.0, scale=-1.0),
                                    incs=[(ac, 1)])
                                set_guard(("qv", slot), ac)
                                op("scalar", lambda e, e2=e2: e.activation(
                                    e_sb[e2][:], m_sb[e2][:], AF.Exp, bias=0.0, scale=-1.0),
                                    waits=gw(("e", e2)), incs=[(ac, 1)])
                                e_done = ac.v
                                wq = [(ac, e_done)] + (gw(("q16", b)) if no == 0 else [])
                                op("vector", lambda e, b=b, e2=e2, no=no: e.scalar_tensor_tensor(
                                    q16[b][:, no * 512:(no + 1) * 512], e_sb[e2][:], -1.0, r_sb[e2][:],
                                    ALU.add, ALU.add), waits=wq, incs=[(po, 1)])
                                set_guard(("rm", e2), po)
                                set_guard(("e", e2), po)
                            else:
                                wv_ = [(pe, mm_done)] + (gw(("v16", 0)) if no == 0 else [])
                                op("scalar", lambda e, slot=slot, no=no: e.activation(
                                    v16[0][:, no * 512:(no + 1) * 512], qv_ps[slot][:], AF.Silu,
                                    bias=0.0, scale=1.0), waits=wv_, incs=[(ac, 1)])
                                set_guard(("qv", slot), ac)
                    q_done_po = po.v
                    v_done_ac = ac.v

                    for h in range(NCH):
                        ws = ([(po, q_done_po), (ac, v_done_ac)] + gw("kvp")) if h == 0 else ()
                        op("tensor", lambda e, b=b, h=h: e.matmul(
                            kv_ps[:, h * P:(h + 1) * P], q16[b][:, h * P:(h + 1) * P],
                            v16[0][:, h * P:(h + 1) * P],
                            start=True, stop=True),
                            waits=ws, incs=([(pe, 1)] if h == NCH - 1 else ()))
                    set_guard(("v16", 0), pe)
                    kvmm = pe.v
                    if t == 0:
                        op("vector", lambda e: e.tensor_copy(kv_acc[:], kv_ps[:]),
                           waits=[(pe, kvmm)], incs=[(dv, 1)])
                    else:
                        op("vector", lambda e: e.tensor_tensor(
                            kv_acc[:], kv_ps[:], kv_acc[:], ALU.add),
                            waits=[(pe, kvmm)], incs=[(dv, 1)])
                    set_guard("kvp", dv)

                    if own:
                        for ch in range(NCH):
                            op("tensor", lambda e, b=b, ch=ch: e.transpose(
                                htr_ps[:, ch * P:(ch + 1) * P], q16[b][:, ch * P:(ch + 1) * P],
                                id16_sb[:]),
                                waits=(([(po, q_done_po)] + gw("htr")) if ch == 0 else ()),
                                incs=([(pe, 1)] if ch == NCH - 1 else ()))
                        qtr_done = pe.v
                        op("vector", lambda e, b=b: e.tensor_copy(qf_sb[b][:], htr_ps[:]),
                           waits=[(pe, qtr_done)] + gw(("qf", 0)), incs=[(dv, 1)])
                        set_guard("htr", dv)
                        qf_done = dv.v
                        op("sync", lambda e, t=t, b=b: e.dma_start(out=qf_s[t], in_=qf_sb[b][:]),
                           waits=[(dv, qf_done)], incs=[(sst, 16)])
                        set_guard(("qf", 0), sst)
                    set_guard(("q16", b), pe)
                    guards[("hf", b)] = [(pe, pe.v), (sst, hf_store_v)]

            scc = Sem(ctx, nc, "scc")
            scc2 = Sem(ctx, nc, "scc2")
            op("gpsimd", lambda e: e.dma_start(out=kv_cc_in, in_=kv_acc[:]),
               waits=[(dv, dv.v)], incs=[(scc, 16)])
            op("gpsimd", lambda e: e.collective_compute(
                "AllReduce", mybir.AluOpType.add,
                replica_groups=[[0, 1], [2, 3], [4, 5], [6, 7]],
                ins=[kv_cc_in.opt()], outs=[kv_cc_out.opt()]),
                waits=[(scc, 16)], incs=[(scc2, 1)])
            op("gpsimd", lambda e: e.dma_start(out=kv_acc[:], in_=kv_cc_out),
               waits=[(scc2, 1)], incs=[(scc, 16)])
            op("vector", lambda e: e.tensor_copy(kv16[:], kv_acc[:]),
               waits=[(scc, 32)], incs=[(dv, 1)])
            kv_done = dv.v
        barrier_A = [(dv, kv_done), (sst, sst.v)]

        if stage == 1:
            op("sync", lambda e: e.dma_start(out=o[0:P, 0:D // 2], in_=kv16.bitcast(F32)[:, 0:D // 2]),
               waits=list(barrier_A) + [(dv, kv_done)], incs=[(sout, 16)])
            op("sync", lambda e: e.nop(), waits=[(sout, sout.v)])
            with nc.Block() as block:
                def runner(eng_name):
                    def f(eng):
                        for e_, fn, ws, incs in plan:
                            if e_ != eng_name:
                                continue
                            for s_, v in ws:
                                eng.wait_ge(s_.h, v)
                            inst = fn(eng)
                            for s_, k in incs:
                                inst.then_inc(s_.h, k)
                    return f
                block.sync(runner("sync"))
                block.gpsimd(runner("gpsimd"))
                block.tensor(runner("tensor"))
                block.vector(runner("vector"))
                block.scalar(runner("scalar"))
            return nc

        with ExitStack() as bctx:
            ACC = bctx.enter_context(nc.sbuf_tensor("ACC", [P, NTS * D], F32))
            h2f = bctx.enter_context(nc.sbuf_tensor("h2f", [P, NTS * D], BF16))
            xB = [bctx.enter_context(nc.sbuf_tensor(f"xB{i}", [P, D], F32)) for i in range(2)]
            stat2 = bctx.enter_context(nc.sbuf_tensor("stat2", [P, 8], F32))
            h2t16 = bctx.enter_context(nc.sbuf_tensor("h2t16", [P, D], BF16))
            astat = bctx.enter_context(nc.sbuf_tensor("astat", [1, TH * 6], F32))
            w_sb = bctx.enter_context(nc.sbuf_tensor("w_sb", [P, NTS * 32], F32))
            rw_sb = bctx.enter_context(nc.sbuf_tensor("rw_sb", [P, 32], F32))
            rwm_sb = bctx.enter_context(nc.sbuf_tensor("rwm_sb", [P, 32], F32))
            mk_sb = bctx.enter_context(nc.sbuf_tensor("mk_sb", [P, 32], F32))
            msc = bctx.enter_context(nc.sbuf_tensor("msc", [P, 8], F32))
            ztmp = bctx.enter_context(nc.sbuf_tensor("ztmp", [P, TH], BF16))

            first_sync = [True]
            first_gp = [True]
            first_pe = [True]

            nslice = 0
            ngup = 0
            for st in range(NST):
                sctx = ExitStack()
                hfB = sctx.enter_context(nc.sbuf_tensor(f"hfB{st}", [P, 2 * D], BF16))
                qfB = sctx.enter_context(nc.sbuf_tensor(f"qfB{st}", [P, 2 * D], BF16))
                u16 = sctx.enter_context(nc.sbuf_tensor(f"u16_{st}", [P, NCH * TH], BF16))
                a16 = sctx.enter_context(nc.sbuf_tensor(f"a16_{st}", [P, NCH * TH], BF16))
                sq16 = sctx.enter_context(nc.sbuf_tensor(f"sq16_{st}", [P, NCH * TH], BF16))
                outf = sctx.enter_context(nc.sbuf_tensor(f"outf{st}", [P, NCH * TH], F32))
                wu_sl = [sctx.enter_context(nc.sbuf_tensor(f"wu_sl{st}_{i}", [P, NCH * P], BF16))
                         for i in range(2)]
                attn_entry = ([(pe, pe.v), (dv, dv.v), (ac, ac.v)]
                              if st > 0 else list(barrier_A))
                first_sync_st = [True]
                first_gp_st = [True]

                def asy():
                    if first_sync_st[0]:
                        first_sync_st[0] = False
                        return list(attn_entry)
                    return []

                def agp():
                    if first_gp_st[0]:
                        first_gp_st[0] = False
                        return list(attn_entry)
                    return []

                for th in range(NTH):
                    tt0 = st * NTS + th * 2
                    with ExitStack() as pctx:
                        pa_f = [pctx.enter_context(nc.psum_tensor(f"pa_{st}_{th}_{i}", [P, 512], F32))
                                for i in range(2)]
                        pu_f = [pctx.enter_context(nc.psum_tensor(f"pu_{st}_{th}_{i}", [P, 512], F32))
                                for i in range(2)]
                        pa = [t[:, 0:TH] for t in pa_f]
                        pu = [t[:, 0:TH] for t in pu_f]
                        pst_a = pctx.enter_context(nc.psum_tensor(f"psta_{st}_{th}", [1, 512], F32))
                        pst_b = pctx.enter_context(nc.psum_tensor(f"pstb_{st}_{th}", [1, 512], F32))
                        pbc = pctx.enter_context(nc.psum_tensor(f"pbc_{st}_{th}", [P, 2 * TH], F32))

                        ldw = asy() + gw("hfB", "qfB")
                        for i, tt in enumerate((tt0, tt0 + 1)):
                            op("sync", lambda e, i=i, tt=tt, hfB=hfB: e.dma_start(
                                out=hfB[:, i * D:(i + 1) * D], in_=hf_s[tt]),
                                waits=(ldw if i == 0 else ()), incs=[(shfq, 16)])
                            op("sync", lambda e, i=i, tt=tt, qfB=qfB: e.dma_start(
                                out=qfB[:, i * D:(i + 1) * D], in_=qf_s[tt]),
                                incs=[(shfq, 16)])
                        hfld = shfq.v

                        for mc in range(NCH):
                            slot = nslice % 2
                            nslice += 1
                            op("gpsimd", lambda e, slot=slot, mc=mc, wu_sl=wu_sl: e.dma_start(
                                out=wu_sl[slot][:], in_=din["wu_t"][mc]),
                                waits=agp() + gw(("wusl", slot)), incs=[(swu[slot], 16)])
                            wld = swu[slot].v
                            for kc in range(NCH):
                                ws = []
                                if kc == 0:
                                    ws = ([(swu[slot], wld), (shfq, hfld)] + gw(("pu", mc % 2))
                                          + (gw("u16") if mc == 0 else []))
                                    if first_pe[0]:
                                        first_pe[0] = False
                                        ws.append((dv, kv_done))
                                op("tensor", lambda e, slot=slot, mc=mc, kc=kc, wu_sl=wu_sl,
                                   pu=pu, hfB=hfB: e.matmul(
                                    pu[mc % 2][:], wu_sl[slot][:, kc * P:(kc + 1) * P],
                                    hfB.rearrange("p (t c j) -> p t c j", t=2, c=NCH)[:, :, kc, :],
                                    start=(kc == 0), stop=(kc == NCH - 1)),
                                    waits=ws, incs=([(pe, 1)] if kc == NCH - 1 else ()))
                            set_guard(("wusl", slot), pe)
                            umm = pe.v
                            wsilu = [(pe, umm)] + (gw("u16") if mc == 0 else [])
                            op("scalar", lambda e, mc=mc, pu=pu, u16=u16: e.activation(
                                u16[:, mc * TH:(mc + 1) * TH], pu[mc % 2][:], AF.Silu,
                                bias=bu_sb[:, mc:mc + 1], scale=1.0),
                                waits=wsilu, incs=[(ac, 1)])
                            set_guard(("pu", mc % 2), ac)
                        set_guard("hfB", pe)

                        for h in range(NCH):
                            ws = [(shfq, hfld)] + gw(("pa", h % 2))
                            op("tensor", lambda e, h=h, pa=pa, qfB=qfB: e.matmul(
                                pa[h % 2][:], kv16[:, h * P:(h + 1) * P],
                                qfB.rearrange("p (t c j) -> p t c j", t=2, c=NCH)[:, :, h, :],
                                start=True, stop=True), waits=ws, incs=[(pe, 1)])
                            amm = pe.v
                            op("vector", lambda e, h=h, pa=pa, a16=a16: e.tensor_copy(
                                a16[:, h * TH:(h + 1) * TH], pa[h % 2][:]),
                                waits=[(pe, amm)], incs=[(dv, 1)])
                            cpv = dv.v
                            if h % 4 == 3:
                                op("scalar", lambda e, h=h, a16=a16, sq16=sq16: e.activation(
                                    sq16[:, (h - 3) * TH:(h + 1) * TH],
                                    a16[:, (h - 3) * TH:(h + 1) * TH],
                                    AF.Square, bias=0.0, scale=1.0),
                                    waits=[(dv, cpv)], incs=[(ac, 1)])
                            set_guard(("pa", h % 2), dv)
                        evac_dv = dv.v
                        evac_ac = ac.v
                        set_guard("qfB", pe)

                        for h in range(NCH):
                            ws = ([(dv, evac_dv), (ac, evac_ac)] + gw("pst")) if h == 0 else ()
                            op("tensor", lambda e, h=h, pst_a=pst_a, a16=a16: e.matmul(
                                pst_a[0:1, 0:TH], ones_c[:, 0:1], a16[:, h * TH:(h + 1) * TH],
                                start=(h == 0), stop=(h == NCH - 1)), waits=ws)
                            op("tensor", lambda e, h=h, pst_b=pst_b, sq16=sq16: e.matmul(
                                pst_b[0:1, 0:TH], ones_c[:, 0:1], sq16[:, h * TH:(h + 1) * TH],
                                start=(h == 0), stop=(h == NCH - 1)),
                                incs=([(pe, 1)] if h == NCH - 1 else ()))
                        stats_mm = pe.v
                        A = astat
                        op("vector", lambda e, pst_a=pst_a: e.tensor_scalar(
                            A[0:1, 0:TH], pst_a[0:1, 0:TH], RD, None, ALU.mult),
                            waits=[(pe, stats_mm)])
                        op("vector", lambda e, pst_b=pst_b: e.tensor_scalar(
                            A[0:1, TH:2 * TH], pst_b[0:1, 0:TH], RD, None, ALU.mult))
                        op("vector", lambda e: e.tensor_tensor(
                            A[0:1, 2 * TH:3 * TH], A[0:1, 0:TH], A[0:1, 0:TH], ALU.mult))
                        op("vector", lambda e: e.scalar_tensor_tensor(
                            A[0:1, 3 * TH:4 * TH], A[0:1, TH:2 * TH], EPS,
                            A[0:1, 2 * TH:3 * TH], ALU.add, ALU.subtract),
                            incs=[(dv, 1)])
                        varB = dv.v
                        set_guard("pst", dv)
                        op("scalar", lambda e: e.activation(
                            A[0:1, 4 * TH:5 * TH], A[0:1, 3 * TH:4 * TH], AF.Ln, bias=0.0,
                            scale=1.0), waits=[(dv, varB)])
                        op("scalar", lambda e: e.activation(
                            A[0:1, 4 * TH:5 * TH], A[0:1, 4 * TH:5 * TH], AF.Exp, bias=0.0,
                            scale=-0.5), incs=[(ac, 1)])
                        rstdB = ac.v
                        op("vector", lambda e: e.scalar_tensor_tensor(
                            A[0:1, 5 * TH:6 * TH], A[0:1, 0:TH], -1.0, A[0:1, 4 * TH:5 * TH],
                            ALU.mult, ALU.mult), waits=[(ac, rstdB)], incs=[(dv, 1)])
                        nmrB = dv.v
                        op("tensor", lambda e, pbc=pbc: e.matmul(
                            pbc[:, 0:TH], ones32[0:1, :], A[0:1, 4 * TH:5 * TH],
                            start=True, stop=True), waits=[(dv, nmrB)] + gw("pbc"))
                        op("tensor", lambda e, pbc=pbc: e.matmul(
                            pbc[:, TH:2 * TH], ones32[0:1, :], A[0:1, 5 * TH:6 * TH],
                            start=True, stop=True), incs=[(pe, 1)])
                        bc_done = pe.v

                        for g4 in range(4):
                            lo, hi = g4 * 4 * TH, (g4 + 1) * 4 * TH
                            ws = [(pe, bc_done), (pe, stats_mm)] if g4 == 0 else ()
                            op("vector", lambda e, lo=lo, hi=hi, a16=a16, sq16=sq16,
                               pbc=pbc: e.tensor_tensor(
                                sq16[:, lo:hi], a16[:, lo:hi],
                                pbc[:, 0:TH].rearrange("p (a n) -> p a n", a=1)
                                .broadcast_to([P, 4, TH]), ALU.mult), waits=ws)
                            op("vector", lambda e, lo=lo, hi=hi, sq16=sq16,
                               pbc=pbc: e.tensor_tensor(
                                sq16[:, lo:hi], sq16[:, lo:hi],
                                pbc[:, TH:2 * TH].rearrange("p (a n) -> p a n", a=1)
                                .broadcast_to([P, 4, TH]), ALU.add))
                        for h in range(NCH):
                            op("vector", lambda e, h=h, sq16=sq16: e.tensor_scalar(
                                sq16[:, h * TH:(h + 1) * TH], sq16[:, h * TH:(h + 1) * TH],
                                lng_sb[:, h:h + 1], lnb_sb[:, h:h + 1], ALU.mult, ALU.add))
                        for g4 in range(4):
                            lo, hi = g4 * 4 * TH, (g4 + 1) * 4 * TH
                            op("vector", lambda e, lo=lo, hi=hi, a16=a16, sq16=sq16,
                               u16=u16: e.tensor_tensor(
                                a16[:, lo:hi], sq16[:, lo:hi], u16[:, lo:hi], ALU.mult),
                                incs=[(dv, 1)])
                        apply_dv = dv.v
                        set_guard("pbc", dv)
                        set_guard("u16", dv)

                        for fo in range(NCH):
                            slot = nslice % 2
                            nslice += 1
                            op("gpsimd", lambda e, slot=slot, fo=fo, wu_sl=wu_sl: e.dma_start(
                                out=wu_sl[slot][:], in_=din["wo_t"][fo]),
                                waits=gw(("wusl", slot)), incs=[(swu[slot], 16)])
                            wld = swu[slot].v
                            for kc in range(NCH):
                                ws = []
                                if kc == 0:
                                    ws = [(swu[slot], wld), (dv, apply_dv)] + gw(("pu", fo % 2))
                                op("tensor", lambda e, slot=slot, fo=fo, kc=kc, wu_sl=wu_sl,
                                   pu=pu, a16=a16: e.matmul(
                                    pu[fo % 2][:], wu_sl[slot][:, kc * P:(kc + 1) * P],
                                    a16[:, kc * TH:(kc + 1) * TH],
                                    start=(kc == 0), stop=(kc == NCH - 1)),
                                    waits=ws, incs=([(pe, 1)] if kc == NCH - 1 else ()))
                            set_guard(("wusl", slot), pe)
                            omm = pe.v
                            op("vector", lambda e, fo=fo, pu=pu, outf=outf: e.tensor_scalar(
                                outf[:, fo * TH:(fo + 1) * TH], pu[fo % 2][:],
                                bo_sb[:, fo:fo + 1], None, ALU.add),
                                waits=[(pe, omm)], incs=[(dv, 1)])
                            set_guard(("pu", fo % 2), dv)
                        wo_dv = dv.v

                    with ExitStack() as pctx2:
                        ptr = pctx2.enter_context(nc.psum_tensor(f"ptr_{st}_{th}", [P, D], F32))
                        h2tr = pctx2.enter_context(nc.psum_tensor(f"h2tr_{st}_{th}", [P, D], BF16))
                        for tsl in range(2):
                            ts_g = th * 2 + tsl
                            xb = ts_g % 2
                            row0 = (st * NTS + ts_g) * P
                            op("sync", lambda e, row0=row0, xb=xb: e.dma_start(
                                out=xB[xb][:], in_=din["x_own"][row0:row0 + P, :]),
                                waits=gw(("xB", xb)), incs=[(sxb[xb], 16)])
                            xbld = sxb[xb].v
                            for fo in range(NCH):
                                ws = ([(dv, wo_dv)] + gw("ptr")) if fo == 0 else ()
                                op("tensor", lambda e, fo=fo, tsl=tsl, ptr=ptr, outf=outf:
                                   e.transpose(
                                    ptr[:, fo * P:(fo + 1) * P],
                                    outf[:, fo * TH + tsl * P: fo * TH + tsl * P + P],
                                    id32_sb[:]),
                                    waits=ws, incs=([(pe, 1)] if fo == NCH - 1 else ()))
                            tr_done = pe.v
                            op("vector", lambda e, ts_g=ts_g, xb=xb, ptr=ptr: e.tensor_tensor(
                                ACC[:, ts_g * D:(ts_g + 1) * D], ptr[:], xB[xb][:], ALU.add),
                                waits=[(pe, tr_done), (sxb[xb], xbld)] + gw(("ACC", ts_g)),
                                incs=[(dv, 1)])
                            set_guard(("xB", xb), dv)
                            set_guard("ptr", dv)
                            acc_dv = dv.v

                            op("scalar", lambda e, ts_g=ts_g: e.activation(
                                h2t16[:], ACC[:, ts_g * D:(ts_g + 1) * D], AF.Copy,
                                bias=0.0, scale=1.0, accum_out=stat2[:, 0:1]),
                                waits=[(dv, acc_dv)] + gw("stat2a") + gw("h2t16"), incs=[(ac, 1)])
                            op("scalar", lambda e, ts_g=ts_g: e.activation(
                                h2t16[:], ACC[:, ts_g * D:(ts_g + 1) * D], AF.Square,
                                bias=0.0, scale=1.0, accum_out=stat2[:, 1:2]), incs=[(ac, 1)])
                            sq2 = ac.v
                            op("vector", lambda e: e.tensor_scalar(
                                stat2[:, 2:3], stat2[:, 0:1], RD, None, ALU.mult),
                                waits=[(ac, sq2)])
                            op("vector", lambda e: e.tensor_scalar(
                                stat2[:, 3:4], stat2[:, 1:2], RD, None, ALU.mult),
                                waits=[(ac, sq2)])
                            op("vector", lambda e: e.tensor_tensor(
                                stat2[:, 4:5], stat2[:, 2:3], stat2[:, 2:3], ALU.mult))
                            op("vector", lambda e: e.scalar_tensor_tensor(
                                stat2[:, 5:6], stat2[:, 3:4], EPS, stat2[:, 4:5],
                                ALU.add, ALU.subtract), incs=[(dv, 1)])
                            var2 = dv.v
                            op("scalar", lambda e: e.activation(
                                stat2[:, 6:7], stat2[:, 6:7] if False else stat2[:, 5:6],
                                AF.Ln, bias=0.0, scale=1.0),
                                waits=[(dv, var2)])
                            op("scalar", lambda e: e.activation(
                                stat2[:, 6:7], stat2[:, 6:7], AF.Exp, bias=0.0, scale=-0.5),
                                incs=[(ac, 1)])
                            rstd2 = ac.v
                            op("vector", lambda e: e.scalar_tensor_tensor(
                                stat2[:, 7:8], stat2[:, 2:3], -1.0, stat2[:, 6:7],
                                ALU.mult, ALU.mult), waits=[(ac, rstd2)])
                            op("vector", lambda e, ts_g=ts_g: e.tensor_scalar(
                                h2t16[:], ACC[:, ts_g * D:(ts_g + 1) * D],
                                stat2[:, 6:7], stat2[:, 7:8], ALU.mult, ALU.add),
                                incs=[(dv, 1)])
                            set_guard("stat2a", dv)
                            h2t_dv = dv.v
                            for ch in range(NCH):
                                ws = ([(dv, h2t_dv)] + gw("h2tr")) if ch == 0 else ()
                                op("tensor", lambda e, ch=ch, h2tr=h2tr: e.transpose(
                                    h2tr[:, ch * P:(ch + 1) * P], h2t16[:, ch * P:(ch + 1) * P],
                                    id16_sb[:]),
                                    waits=ws, incs=([(pe, 1)] if ch == NCH - 1 else ()))
                            h2tr_pe = pe.v
                            set_guard("h2t16", pe)
                            op("vector", lambda e, ts_g=ts_g, h2tr=h2tr: e.tensor_copy(
                                h2f[:, ts_g * D:(ts_g + 1) * D], h2tr[:]),
                                waits=[(pe, h2tr_pe)] + gw(("h2f", ts_g)), incs=[(dv, 1)])
                            set_guard("h2tr", dv)
                            set_guard(("h2f_rdy", ts_g), dv)

                    for k in (("pa", 0), ("pa", 1), ("pu", 0), ("pu", 1), "pst", "pbc",
                              "ptr", "h2tr"):
                        set_guard(k, dv)

                sctx.close()
                if stage == 2:
                    for ts8 in range(NTS):
                        row0 = (st * NTS + ts8) * P
                        op("sync", lambda e, ts8=ts8, row0=row0: e.dma_start(
                            out=o[row0:row0 + P, :], in_=ACC[:, ts8 * D:(ts8 + 1) * D]),
                            waits=[(dv, dv.v)], incs=[(sout, 16)])
                        set_guard(("ACC", ts8), sout)
                    for ts_g in range(NTS):
                        set_guard(("h2f", ts_g), dv)
                    continue
                ffn_entry = [(pe, pe.v), (dv, dv.v), (ac, ac.v)]
                first_gp_f = [True]

                def fgp():
                    if first_gp_f[0]:
                        first_gp_f[0] = False
                        return list(ffn_entry)
                    return []

                with ExitStack() as fctx:
                    gated = fctx.enter_context(nc.sbuf_tensor(f"gated{st}", [P, GCH * 1024], BF16))
                    wgu_sl = [fctx.enter_context(
                        nc.sbuf_tensor(f"wgu_sl{st}_{i}", [P, 2 * NCH * P], BF16))
                        for i in range(2)]
                    wd_sb = [fctx.enter_context(
                        nc.sbuf_tensor(f"wd_sb{st}_0", [P, 24 * 512], BF16))]
                    silu_scr = fctx.enter_context(nc.sbuf_tensor(f"silu_scr{st}", [P, 512], BF16))
                    pg = [fctx.enter_context(nc.psum_tensor(f"pg_{st}_{i}", [P, 512], F32))
                          for i in range(2)]
                    pu2 = [fctx.enter_context(nc.psum_tensor(f"pu2_{st}_{i}", [P, 512], F32))
                           for i in range(2)]
                    pd = [fctx.enter_context(nc.psum_tensor(f"pd_{st}_{i}", [P, 512], F32))
                          for i in range(3)]
                    prt_f = fctx.enter_context(nc.psum_tensor(f"prt_{st}", [P, 512], F32))
                    prt = prt_f[:, 0:32]
                    for k in (("pg", 0), ("pg", 1), ("pu2", 0), ("pu2", 1),
                              ("pd", 0), ("pd", 1), ("pd", 2), "prt"):
                        set_guard(k, dv)

                    for ts_g in range(NTS):
                        for kc in range(NCH):
                            ws = (gw(("h2f_rdy", ts_g)) + gw("prt")) if kc == 0 else ()
                            op("tensor", lambda e, ts_g=ts_g, kc=kc, prt=prt: e.matmul(
                                prt, h2f[:, ts_g * D + kc * P: ts_g * D + (kc + 1) * P],
                                gate16[:, kc * 32:(kc + 1) * 32],
                                start=(kc == 0), stop=(kc == NCH - 1)),
                                waits=ws, incs=([(pe, 1)] if kc == NCH - 1 else ()))
                        rmm = pe.v
                        op("vector", lambda e, prt=prt: e.reduce_max(
                            msc[:, 0:1], prt, axis=AX.X),
                            waits=[(pe, rmm)] + gw("router_bufs"), incs=[(dv, 1)])
                        rmax1 = dv.v
                        op("vector", lambda e: e.tensor_scalar(
                            msc[:, 1:2], msc[:, 0:1], -1.0, None, ALU.mult),
                            waits=[(dv, rmax1)], incs=[(dv, 1)])
                        nm1 = dv.v
                        op("scalar", lambda e, prt=prt: e.activation(
                            rw_sb[:], prt, AF.Exp, bias=msc[:, 1:2], scale=1.0),
                            waits=[(dv, nm1)], incs=[(ac, 1)])
                        set_guard("prt", ac)
                        expd = ac.v
                        op("vector", lambda e: e.tensor_copy(rwm_sb[:], rw_sb[:]),
                           waits=[(ac, expd)], incs=[(dv, 1)])
                        for _ in range(3):
                            wcp = dv.v
                            op("vector", lambda e: e.reduce_max(msc[:, 2:3], rwm_sb[:], axis=AX.X),
                               waits=[(dv, wcp)], incs=[(dv, 1)])
                            rmx = dv.v
                            op("vector", lambda e: e.tensor_single_scalar(
                                mk_sb[:], rwm_sb[:], msc[:, 2:3], ALU.is_ge),
                                waits=[(dv, rmx)])
                            op("vector", lambda e: e.scalar_tensor_tensor(
                                rwm_sb[:], mk_sb[:], -1e30, rwm_sb[:], ALU.mult, ALU.add),
                                incs=[(dv, 1)])
                        wcp4 = dv.v
                        op("vector", lambda e: e.reduce_max(msc[:, 2:3], rwm_sb[:], axis=AX.X),
                           waits=[(dv, wcp4)], incs=[(dv, 1)])
                        rmx4 = dv.v
                        op("vector", lambda e: e.tensor_single_scalar(
                            mk_sb[:], rw_sb[:], msc[:, 2:3], ALU.is_ge),
                            waits=[(dv, rmx4)])
                        op("vector", lambda e: e.tensor_tensor(
                            rwm_sb[:], rw_sb[:], mk_sb[:], ALU.mult), incs=[(dv, 1)])
                        wwr = dv.v
                        op("vector", lambda e: e.reduce_sum(msc[:, 3:4], rwm_sb[:], axis=AX.X),
                           waits=[(dv, wwr)], incs=[(dv, 1)])
                        rs4 = dv.v
                        op("vector", lambda e: e.reciprocal(msc[:, 4:5], msc[:, 3:4]),
                           waits=[(dv, rs4)], incs=[(dv, 1)])
                        rrec = dv.v
                        op("vector", lambda e, ts_g=ts_g: e.tensor_scalar(
                            w_sb[:, ts_g * 32:(ts_g + 1) * 32], rwm_sb[:], msc[:, 4:5], None,
                            ALU.mult), waits=[(dv, rrec)], incs=[(dv, 1)])
                    set_guard("router_bufs", dv)
                    w_done = dv.v

                    ev_done = {}
                    for ex in range(NE):
                        dslot = 0
                        op("gpsimd", lambda e, ex=ex, dslot=dslot, wd_sb=wd_sb: e.dma_start(
                            out=wd_sb[dslot][:], in_=din["wd_t"][ex]),
                            waits=fgp() + gw(("wd", dslot)), incs=[(swd[dslot], 16)])
                        wdld = swd[dslot].v

                        for mcp in range(GCH):
                            gslot = ngup % 2
                            ngup += 1
                            op("gpsimd", lambda e, ex=ex, mcp=mcp, gslot=gslot,
                               wgu_sl=wgu_sl: e.dma_start(
                                out=wgu_sl[gslot][:, 0:NCH * P], in_=din["gup_t"][ex, mcp]),
                                waits=gw(("wgu", gslot)), incs=[(swg[gslot], 16)])
                            op("gpsimd", lambda e, ex=ex, mcp=mcp, gslot=gslot,
                               wgu_sl=wgu_sl: e.dma_start(
                                out=wgu_sl[gslot][:, NCH * P:2 * NCH * P],
                                in_=din["gup_t"][ex, mcp + GCH]),
                                incs=[(swg[gslot], 16)])
                            gld = swg[gslot].v
                            for th2 in range(2):
                                ps_i = (mcp * 2 + th2) % 2
                                for kc in range(NCH):
                                    ws = []
                                    if kc == 0:
                                        ws = [(swg[gslot], gld)] + gw(("pg", ps_i))
                                        if ex == 0 and mcp == 0 and th2 == 0:
                                            ws += [g for t2 in range(NTS)
                                                   for g in gw(("h2f_rdy", t2))]
                                    op("tensor", lambda e, gslot=gslot, ps_i=ps_i, kc=kc,
                                       th2=th2, wgu_sl=wgu_sl, pg=pg: e.matmul(
                                        pg[ps_i][:], wgu_sl[gslot][:, kc * P:(kc + 1) * P],
                                        h2f.rearrange("p (t c j) -> p t c j", t=NTS, c=NCH)[
                                            :, th2 * 4:(th2 + 1) * 4, kc, :],
                                        start=(kc == 0), stop=(kc == NCH - 1)), waits=ws)
                                for kc in range(NCH):
                                    ws = gw(("pu2", ps_i)) if kc == 0 else ()
                                    op("tensor", lambda e, gslot=gslot, ps_i=ps_i, kc=kc,
                                       th2=th2, wgu_sl=wgu_sl, pu2=pu2: e.matmul(
                                        pu2[ps_i][:],
                                        wgu_sl[gslot][:, NCH * P + kc * P: NCH * P + (kc + 1) * P],
                                        h2f.rearrange("p (t c j) -> p t c j", t=NTS, c=NCH)[
                                            :, th2 * 4:(th2 + 1) * 4, kc, :],
                                        start=(kc == 0), stop=(kc == NCH - 1)),
                                        waits=ws, incs=([(pe, 1)] if kc == NCH - 1 else ()))
                                gumm = pe.v
                                set_guard(("wgu", gslot), pe)
                                op("scalar", lambda e, ps_i=ps_i, pg=pg,
                                   silu_scr=silu_scr: e.activation(
                                    silu_scr[:], pg[ps_i][:], AF.Silu, bias=0.0, scale=1.0),
                                    waits=[(pe, gumm)] + gw("silu_scr"), incs=[(ac, 1)])
                                set_guard(("pg", ps_i), ac)
                                sdone = ac.v
                                op("vector", lambda e, ps_i=ps_i, mcp=mcp, th2=th2, pu2=pu2,
                                   gated=gated, silu_scr=silu_scr: e.tensor_tensor(
                                    gated[:, mcp * 1024 + th2 * 512: mcp * 1024 + (th2 + 1) * 512],
                                    pu2[ps_i][:], silu_scr[:], ALU.mult),
                                    waits=[(ac, sdone)], incs=[(dv, 1)])
                                set_guard(("pu2", ps_i), dv)
                                set_guard("silu_scr", dv)
                        gated_dv = dv.v

                        for ts8 in range(NTS):
                            th2 = ts8 // 4
                            j0 = (ts8 % 4) * P
                            for no in range(4):
                                ds = (ts8 * 4 + no) % 3
                                for kc in range(GCH):
                                    ws = []
                                    if kc == 0:
                                        ws = ([(swd[dslot], wdld), (dv, gated_dv)]
                                              + gw(("pd", ds)))
                                    op("tensor", lambda e, ds=ds, kc=kc, no=no, th2=th2,
                                       j0=j0, dslot=dslot, gated=gated, wd_sb=wd_sb,
                                       pd=pd: e.matmul(
                                        pd[ds][:],
                                        gated[:, kc * 1024 + th2 * 512 + j0:
                                              kc * 1024 + th2 * 512 + j0 + P],
                                        wd_sb[dslot][:, (no * GCH + kc) * 512:
                                                     (no * GCH + kc + 1) * 512],
                                        start=(kc == 0), stop=(kc == GCH - 1)),
                                        waits=ws, incs=([(pe, 1)] if kc == GCH - 1 else ()))
                                dmm = pe.v
                                wsx = [(pe, dmm)]
                                if ex == 0 and no == 0:
                                    wsx.append((dv, w_done))
                                op("vector", lambda e, ds=ds, ts8=ts8, no=no, ex=ex,
                                   pd=pd: e.scalar_tensor_tensor(
                                    ACC[:, ts8 * D + no * 512: ts8 * D + (no + 1) * 512],
                                    pd[ds][:], w_sb[:, ts8 * 32 + ex: ts8 * 32 + ex + 1],
                                    ACC[:, ts8 * D + no * 512: ts8 * D + (no + 1) * 512],
                                    ALU.mult, ALU.add),
                                    waits=wsx, incs=[(dv, 1)])
                                set_guard(("pd", ds), dv)
                                if ex == NE - 1 and no == 3:
                                    ev_done[ts8] = dv.v
                        set_guard(("wd", dslot), pe)

                    for ts8 in range(NTS):
                        row0 = (st * NTS + ts8) * P
                        op("sync", lambda e, ts8=ts8, row0=row0: e.dma_start(
                            out=o[row0:row0 + P, :], in_=ACC[:, ts8 * D:(ts8 + 1) * D]),
                            waits=[(dv, ev_done[ts8])], incs=[(sout, 16)])
                        set_guard(("ACC", ts8), sout)
                    for ts_g in range(NTS):
                        set_guard(("h2f", ts_g), pe)

            op("sync", lambda e: e.nop(), waits=[(sout, sout.v)])

        build_nc._last_plan = plan
        build_nc._last_nc = nc
        fl_v = Sem(ctx, nc, "fl_v")
        fl_a = Sem(ctx, nc, "fl_a")
        fl_p = Sem(ctx, nc, "fl_p")
        flush_sem = {"vector": fl_v, "scalar": fl_a, "pool": fl_p}
        with nc.Block() as block:
            def runner(eng_name):
                fs = flush_sem.get(eng_name)
                names = {eng_name, "pool"} if eng_name == "gpsimd" else {eng_name}
                def f(eng):
                    for e_, fn, ws, incs in plan:
                        if e_ not in names:
                            continue
                        if e_ == "pool":
                            fs2 = flush_sem["pool"]
                        else:
                            fs2 = fs if e_ != "gpsimd" else None
                        for s, v in ws:
                            eng.wait_ge(s.h, v)
                        inst = fn(eng)
                        if fs2 is not None:
                            fs2.v += 1
                            inst.then_inc(fs2.h, 1)
                            eng.wait_ge(fs2.h, fs2.v)
                            if incs:
                                inst = eng.nop()
                        for s, k in incs:
                            inst.then_inc(s.h, k)
                return f
            block.sync(runner("sync"))
            block.gpsimd(runner("gpsimd"))
            block.tensor(runner("tensor"))
            block.vector(runner("vector"))
            block.scalar(runner("scalar"))
    return nc


_NC_CACHE = {}


def _get_nc():
    if "nc" not in _NC_CACHE:
        _NC_CACHE["nc"] = build_nc()
    return _NC_CACHE["nc"]


def kernel(L0_attn_x, L0_attn_Wqk, L0_attn_bqk, L0_attn_Wv, L0_attn_bv,
           L0_attn_Wu, L0_attn_bu, L0_attn_Wo, L0_attn_bo, L0_attn_ln_g,
           L0_attn_ln_b, L0_ffn_gate, L0_ffn_gate_up_proj, L0_ffn_down_proj):
    c = np.ascontiguousarray
    x = np.asarray(L0_attn_x).reshape(8 * 4096, D)
    wqk = c(np.asarray(L0_attn_Wqk))
    wv = c(np.asarray(L0_attn_Wv))
    wu_t = c(np.asarray(L0_attn_Wu).reshape(NCH, P, NCH, P).transpose(2, 1, 0, 3))
    wo_t = c(np.asarray(L0_attn_Wo).reshape(NCH, P, NCH, P).transpose(2, 1, 0, 3))
    gup_t = c(np.asarray(L0_ffn_gate_up_proj).reshape(NE, NCH, P, 12, P)
              .transpose(0, 3, 2, 1, 4))
    wd_t = c(np.asarray(L0_ffn_down_proj).reshape(NE, GCH, P, 4, 512)
             .transpose(0, 2, 3, 1, 4))
    gate_t = c(np.asarray(L0_ffn_gate).reshape(NCH, P, 32).transpose(1, 0, 2))
    bqk_r = c(np.asarray(L0_attn_bqk).reshape(1, D))
    bv_r = c(np.asarray(L0_attn_bv).reshape(1, D))
    bo_c = c(np.asarray(L0_attn_bo).reshape(NCH, P).T)
    bu_c = c(np.asarray(L0_attn_bu).reshape(NCH, P).T)
    lng_c = c(np.asarray(L0_attn_ln_g).reshape(NCH, P).T)
    lnb_c = c(np.asarray(L0_attn_ln_b).reshape(NCH, P).T)
    ones_r16 = np.ones((1, P), dtype=ml_dtypes.bfloat16)
    ones_c16 = np.ones((P, 1), dtype=ml_dtypes.bfloat16)
    ones_r32 = np.ones((1, P), dtype=np.float32)
    id16 = np.eye(P, dtype=ml_dtypes.bfloat16)
    id32 = np.eye(P, dtype=np.float32)

    shared = dict(wqk=wqk, wv=wv, wu_t=wu_t, wo_t=wo_t, gup_t=gup_t, wd_t=wd_t,
                  gate_t=gate_t, bqk_r=bqk_r, bv_r=bv_r, bo_c=bo_c, bu_c=bu_c,
                  lng_c=lng_c, lnb_c=lnb_c, ones_r16=ones_r16, ones_c16=ones_c16,
                  ones_r32=ones_r32, id16=id16, id32=id32)
    in_maps = []
    for core in range(8):
        m = dict(shared)
        m["x_own"] = c(x[core * 4096:(core + 1) * 4096])
        in_maps.append(m)

    nc = _get_nc()
    res = run_bass_kernel_spmd(nc, in_maps, core_ids=list(range(8)))
    out = np.concatenate([np.asarray(res.results[core]["o"]) for core in range(8)], axis=0)
    return out.reshape(4, 8192, D).astype(np.float32, copy=False)
